# version 11
# baseline (speedup 1.0000x reference)
import numpy as np
import contextlib
import ml_dtypes
import concourse.bass as bass
import concourse.mybir as mybir
from concourse.bass_utils import run_bass_kernel_spmd

F32 = mybir.dt.float32
BF16 = mybir.dt.bfloat16
AF = mybir.ActivationFunctionType
ALU = mybir.AluOpType
AX = mybir.AxisListType

NCORES = 8
D = 1024
H = 16
DH = 64
NFREQ = 32
SPW = 512
EPS = 1e-5
NEGBIG = -30000.0

ENGINES = ("sync", "tensor", "vector", "scalar", "gpsimd")



def _qk_perm():
    perm = np.zeros(3 * D, dtype=np.int64)
    r = 0
    for base in (0, D):
        for jb in range(4):
            for eo in (0, 1):
                for hm in range(4):
                    for i in range(NFREQ):
                        perm[r] = base + 64 * (4 * jb + hm) + 2 * i + eo
                        r += 1
    perm[2 * D:] = np.arange(2 * D, 3 * D)
    return perm


def _theta_tables(T):
    inv_freq = 1.0 / 10000.0 ** (np.arange(0, DH, 2, dtype=np.float64) / DH)
    pos = np.arange(T, dtype=np.float64)
    th = pos[None, :] * inv_freq[:, None]
    out = []
    for shift in (0.0, np.pi / 2):
        red = np.mod(th + shift + np.pi, 2 * np.pi) - np.pi
        hi = np.clip(red.astype(np.float32), -3.1415925, 3.1415925)
        lo = (red - hi.astype(np.float64)).astype(np.float32)
        out += [np.tile(hi, (4, 1)).astype(np.float32),
                np.tile(lo, (4, 1)).astype(np.float32)]
    return out


def _host_consts(T):
    c = {}
    c["ident"] = np.eye(128, dtype=np.float32).astype(ml_dtypes.bfloat16)
    c["ones_bf"] = np.ones((128, 128), np.float32).astype(ml_dtypes.bfloat16)
    c["negmask"] = np.triu(np.ones((128, 128), np.float32), k=1) * NEGBIG
    xdg = np.zeros((32, 512), np.float32)
    for d in range(32):
        xdg[d, 128 * (d // 8) + 16 * (d % 8):128 * (d // 8) + 16 * (d % 8) + 16] = 1.0
    c["xdg"] = xdg
    xg = np.zeros((16, 128), np.float32)
    for dl in range(8):
        for g in range(16):
            xg[g, 16 * dl + g] = 1.0
    c["xg"] = xg
    th = _theta_tables(T)
    c["th_sin_hi"], c["th_sin_lo"], c["th_cos_hi"], c["th_cos_lo"] = th
    return c



class Sched:
    def __init__(self, nc, es):
        self.nc = nc
        self.q = {e: [] for e in ENGINES}
        self.cnt = {}
        self.sems = {}
        self.es = es

    def sem(self, name):
        if name not in self.sems:
            self.sems[name] = self.es.enter_context(self.nc.semaphore(name))
            self.cnt[name] = 0
        return self.sems[name]

    def now(self, name):
        self.sem(name)
        return self.cnt[name]

    def add(self, eng, fn, wait=(), inc=None, inc_by=1):
        wait = [(self.sem(s), v) for (s, v) in wait if v > 0]
        itok = None
        if inc is not None:
            self.sem(inc)
            self.cnt[inc] += inc_by
            itok = (self.sems[inc], inc_by)
        self.q[eng].append((fn, wait, itok))
        return self.cnt[inc] if inc else None

    def dma(self, eng, out_ap, in_ap, wait=(), inc=None):
        return self.add(eng, lambda e: e.dma_start(out=out_ap, in_=in_ap),
                        wait=wait, inc=inc, inc_by=16)

    def emit(self, block):
        for eng in ENGINES:
            ops = list(self.q[eng])

            def make(eng_ops):
                def body(e):
                    for fn, wait, itok in eng_ops:
                        for s, v in wait:
                            e.wait_ge(s, v)
                        instr = fn(e)
                        if itok is not None:
                            instr.then_inc(itok[0], itok[1])
                return body
            getattr(block, eng)(make(ops))



def build_nc(NB, T, stop=None, dbg=False):
    TT = T // 128
    NSP = T // SPW
    nc = bass.Bass("TRN2", target_bir_lowering=False,
                   dynamic_dma_scratch_size=16384)
    d_dbg = {}
    if dbg:
        d_dbg["dOn"] = nc.dram_tensor("dOn", [128, 8 * T], BF16,
                                      kind="ExternalOutput").ap()
        d_dbg["dmxl"] = nc.dram_tensor("dmxl", [128, 16], F32,
                                       kind="ExternalOutput").ap()
        d_dbg["dptil"] = nc.dram_tensor("dptil", [128, T], BF16,
                                        kind="ExternalOutput").ap()
        d_dbg["dptT"] = nc.dram_tensor("dptT", [128, T], BF16,
                                       kind="ExternalOutput").ap()
        d_dbg["dmask"] = nc.dram_tensor("dmask", [128, 128], F32,
                                        kind="ExternalOutput").ap()

    d_x = nc.dram_tensor("x", [NB, T, D], F32, kind="ExternalInput").ap()
    d_wqkvT = nc.dram_tensor("wqkvT", [D, 3 * D], F32, kind="ExternalInput").ap()
    d_bcols = nc.dram_tensor("bcols", [128, 16], F32, kind="ExternalInput").ap()
    d_brow = nc.dram_tensor("brow", [1, 3 * D], F32, kind="ExternalInput").ap()
    d_woutT = nc.dram_tensor("woutT", [D, D], F32, kind="ExternalInput").ap()
    d_bout = nc.dram_tensor("bout", [1, D], F32, kind="ExternalInput").ap()
    d_g1 = {k: nc.dram_tensor(f"g1{k}", [4, 32, 32], F32, kind="ExternalInput").ap()
            for k in "ri"}
    d_g2 = {k: nc.dram_tensor(f"g2{k}", [4, 16, 16], F32, kind="ExternalInput").ap()
            for k in "ri"}
    d_fb = {k: nc.dram_tensor(f"fb{k}", [1, 512], F32, kind="ExternalInput").ap()
            for k in "ri"}
    d_c = {k: nc.dram_tensor(k, list(v.shape),
                             BF16 if v.dtype == ml_dtypes.bfloat16 else F32,
                             kind="ExternalInput").ap()
           for k, v in _host_consts(T).items()}
    d_y = nc.dram_tensor("y", [NB, T, D], F32, kind="ExternalOutput").ap()

    es = contextlib.ExitStack()
    S = Sched(nc, es)

    def sb(stack, name, shape, dt=F32, side="left"):
        return stack.enter_context(nc.sbuf_tensor(name, shape, dt, side=side))

    with es:
        ident = sb(es, "identS", [128, 128], BF16)
        ones_bf = sb(es, "onesS", [128, 128], BF16)
        negmask = sb(es, "negmaskS", [128, 128], F32)
        bcols = sb(es, "bcolsS", [128, 16], F32)
        boutrow = sb(es, "boutrS", [1, D], BF16)
        fbrow = sb(es, "fbrowS", [1, 1024], BF16)
        wt_r = sb(es, "wtrS", [128, 2048], BF16)
        wt_i = sb(es, "wtiS", [128, 2048], BF16)
        wt_ni = sb(es, "wtniS", [128, 2048], BF16)
        bvb = sb(es, "bvbS", [128, D], F32)
        bvrow = sb(es, "bvrowS", [1, 1024], BF16)
        lns = [sb(es, f"lnsS{p}", [128, 16], F32) for p in range(2)]
        sq_junk = sb(es, "sqjS", [128, D], BF16)
        rs = [sb(es, f"rsS{i}", [128, SPW], F32) for i in range(2)]
        xt_in = [sb(es, f"xtinS{p}", [128, D], F32) for p in range(2)]
        pb = [es.enter_context(nc.psum_tensor(f"pb{i}", [128, 512], F32))
              for i in range(8)]

        for ap_sb, key in ((ident, "ident"), (ones_bf, "ones_bf"),
                           (negmask, "negmask")):
            S.dma("sync", ap_sb.ap(), d_c[key], inc="s_ld")
        S.dma("sync", bcols.ap(), d_bcols, inc="s_ld")

        setup_es = contextlib.ExitStack()
        brow_f = sb(setup_es, "browF", [1, 3 * D], F32, side="right")
        bout_f = sb(setup_es, "boutF", [1, D], F32, side="right")
        fb_f = sb(setup_es, "fbF", [1, 1024], F32, side="right")
        g1tmp = sb(setup_es, "g1tmpS", [32, 128], F32, side="right")
        g2tmp = sb(setup_es, "g2tmpS", [16, 64], F32, side="right")
        g1rep = sb(setup_es, "g1repS", [32, 2048], F32, side="right")
        vcs = sb(setup_es, "vcsS", [128, 2048], F32, side="right")
        wacc = sb(setup_es, "waccS", [128, 512], F32, side="right")
        wtmp = sb(setup_es, "wtmpS", [128, 512], F32, side="right")
        xdg = sb(setup_es, "xdgS", [32, 512], F32, side="right")
        xg = sb(setup_es, "xgS", [16, 128], F32, side="right")

        S.dma("sync", brow_f.ap(), d_brow, inc="s_ld")
        S.dma("sync", bout_f.ap(), d_bout, inc="s_ld")
        S.dma("sync", fb_f.ap()[:, 0:512], d_fb["r"], inc="s_ld")
        S.dma("sync", fb_f.ap()[:, 512:1024], d_fb["i"], inc="s_ld")
        S.dma("sync", xdg.ap(), d_c["xdg"], inc="s_ld")
        S.dma("sync", xg.ap(), d_c["xg"], inc="s_ld")
        nld = S.now("s_ld")

        S.add("vector", lambda e: e.tensor_copy(boutrow.ap(), bout_f.ap()),
              wait=[("s_ld", nld)], inc="s_cst")
        S.add("vector", lambda e: e.tensor_copy(fbrow.ap(), fb_f.ap()), inc="s_cst")

        for mi, mk in enumerate("ri"):
            wdst = wt_r if mi == 0 else wt_i
            for c in range(4):
                wg = [("s_wbrep", S.now("s_wbrep")),
                      ("s_wbmm", S.now("s_wbmm"))] if c == 0 else ()
                S.dma("sync", g1tmp.ap()[:, 32 * c:32 * c + 32], d_g1[mk][c],
                      wait=wg, inc="s_wbld")
                S.dma("sync", g2tmp.ap()[:, 16 * c:16 * c + 16], d_g2[mk][c],
                      inc="s_wbld")
            nldg = S.now("s_wbld")
            for c in range(4):
                for ei in range(16):
                    wg = [("s_wbld", nldg),
                          ("s_wbmm", S.now("s_wbmm"))] if (c == 0 and ei == 0) else ()
                    S.add("vector",
                          lambda e, c=c, ei=ei: e.tensor_copy(
                              bass.AP(g1rep, 512 * c + ei, [[2048, 32], [16, 32]]),
                              g1tmp.ap()[:, 32 * c:32 * c + 32]),
                          wait=wg, inc="s_wbrep")
            nrep = S.now("s_wbrep")
            for c in range(4):
                S.add("tensor",
                      lambda e, c=c: e.matmul(
                          pb[7][0:128, 0:16], xg.ap()[:],
                          g2tmp.ap()[:, 16 * c:16 * c + 16],
                          start=True, stop=True),
                      wait=[("s_ld", nld), ("s_wbld", nldg),
                            ("s_wbv", S.now("s_wbv"))],
                      inc="s_wbmm")
                nmmv = S.now("s_wbmm")
                for bi in range(32):
                    S.add("vector",
                          lambda e, c=c, bi=bi: e.tensor_copy(
                              bass.AP(vcs, 512 * c + 16 * bi, [[2048, 128], [1, 16]]),
                              pb[7][0:128, 0:16]),
                          wait=[("s_wbmm", nmmv),
                                ("s_wbv", S.now("s_wbv"))] if bi == 0 else (),
                          inc="s_wbv")
            for ic in range(4):
                for c in range(4):
                    S.add("tensor",
                          lambda e, ic=ic, c=c: e.matmul(
                              pb[6][0:128, 0:512],
                              xdg.ap()[:, 128 * ic:128 * ic + 128],
                              g1rep.ap()[:, 512 * c:512 * c + 512],
                              start=True, stop=True),
                          wait=[("s_wbrep", nrep), ("s_wbv", S.now("s_wbv"))],
                          inc="s_wbmm")
                    nmmu = S.now("s_wbmm")
                    if c == 0:
                        S.add("vector",
                              lambda e, c=c: e.tensor_mul(
                                  wacc.ap(), pb[6][0:128, 0:512],
                                  vcs.ap()[:, 512 * c:512 * c + 512]),
                              wait=[("s_wbmm", nmmu),
                                    ("s_wbv", S.now("s_wbv"))], inc="s_wbv")
                    else:
                        S.add("vector",
                              lambda e, c=c: e.tensor_mul(
                                  wtmp.ap(), pb[6][0:128, 0:512],
                                  vcs.ap()[:, 512 * c:512 * c + 512]),
                              wait=[("s_wbmm", nmmu),
                                    ("s_wbv", S.now("s_wbv"))], inc="s_wbv")
                        S.add("vector",
                              lambda e: e.tensor_add(wacc.ap(), wacc.ap(),
                                                     wtmp.ap()),
                              wait=[("s_wbv", S.now("s_wbv"))], inc="s_wbv")
                S.add("vector",
                      lambda e, ic=ic, wdst=wdst: e.tensor_copy(
                          wdst.ap()[:, 512 * ic:512 * ic + 512], wacc.ap()),
                      wait=[("s_wbv", S.now("s_wbv"))], inc="s_wbv")
        nwb_mm = S.now("s_wbmm")
        nwb_v = S.now("s_wbv")
        S.add("vector", lambda e: e.tensor_scalar_mul(wt_ni.ap(), wt_i.ap(), -1.0),
              wait=[("s_wbv", nwb_v)], inc="s_cst")

        S.add("vector", lambda e: e.tensor_copy(bvrow.ap(),
                                                brow_f.ap()[0:1, 2 * D:3 * D]),
              wait=[("s_ld", nld)], inc="s_cst")
        ncst_bv = S.now("s_cst")
        for j in range(2):
            S.add("tensor",
                  lambda e, j=j: e.matmul(
                      pb[7][0:128, 0:512], ones_bf.ap()[0:1, 0:128],
                      bvrow.ap()[0:1, 512 * j:512 * j + 512],
                      start=True, stop=True),
                  wait=[("s_cst", ncst_bv), ("s_bvbe", S.now("s_bvbe")),
                        ("s_wbv", nwb_v)],
                  inc="s_bvb")
            S.add("vector",
                  lambda e, j=j: e.tensor_copy(bvb.ap()[:, 512 * j:512 * j + 512],
                                               pb[7][0:128, 0:512]),
                  wait=[("s_bvb", S.now("s_bvb"))], inc="s_bvbe")
        setup_done = dict(wbv=nwb_v, wbmm=nwb_mm, bvbe=S.now("s_bvbe"),
                          cst=S.now("s_cst"), trig=S.now("s_trig"))
        setup_es.close()

        def emit_batch(b, out_total):
            bat_es = contextlib.ExitStack()
            QE = sb(bat_es, f"QE{b}", [128, 4 * T], BF16)
            QO = sb(bat_es, f"QO{b}", [128, 4 * T], BF16)
            KE = sb(bat_es, f"KE{b}", [128, 4 * T], BF16)
            KO = sb(bat_es, f"KO{b}", [128, 4 * T], BF16)
            Vsb = sb(bat_es, f"V{b}", [128, TT * 1024], BF16)

            r1_es = contextlib.ExitStack()
            c4 = sb(r1_es, f"c4_{b}", [128, T], BF16, side="right")
            s4 = sb(r1_es, f"s4_{b}", [128, T], BF16, side="right")
            r2_es = contextlib.ExitStack()
            tstage = {k: sb(r2_es, f"t{b}_{k}", [128, T], F32, side="right")
                      for k in ("th_sin_hi", "th_sin_lo", "th_cos_hi",
                                "th_cos_lo")}
            wtheta = [("s_x3", S.now("s_x3")), ("s_onmv0", S.now("s_onmv0")),
                      ("s_onmv1", S.now("s_onmv1")), ("s_ffn", S.now("s_ffn")),
                      ("s_dbg", S.now("s_dbg"))]
            if b == 0:
                wtheta = [("s_ld", nld), ("s_cst", S.now("s_cst")),
                          ("s_wbmm", nwb_mm), ("s_wbv", nwb_v),
                          ("s_bvb", S.now("s_bvb"))]
            for k in tstage:
                S.dma("sync", tstage[k].ap(), d_c[k], wait=wtheta, inc="s_thld")
            nthld = S.now("s_thld")
            S.add("vector", lambda e, t=tstage: e.tensor_add(
                t["th_sin_hi"].ap(), t["th_sin_hi"].ap(), t["th_sin_lo"].ap()),
                wait=[("s_thld", nthld)], inc="s_th")
            S.add("vector", lambda e, t=tstage: e.tensor_add(
                t["th_cos_hi"].ap(), t["th_cos_hi"].ap(), t["th_cos_lo"].ap()),
                inc="s_th")
            S.add("scalar", lambda e, t=tstage, s4=s4: e.activation(
                out=s4.ap(), in_=t["th_sin_hi"].ap(), func=AF.Sin),
                wait=[("s_th", S.now("s_th"))], inc="s_trig")
            S.add("scalar", lambda e, t=tstage, c4=c4: e.activation(
                out=c4.ap(), in_=t["th_cos_hi"].ap(), func=AF.Sin), inc="s_trig")
            n_trig = S.now("s_trig")
            r2_es.close()
            if stop == "trig":
                r1_es.close()
                bat_es.close()
                return out_total

            p1_es = contextlib.ExitStack()
            xhatT = sb(p1_es, f"xhatT{b}", [128, 8 * T], BF16, side="right")
            xh1 = [sb(p1_es, f"xh1_{b}_0", [128, D], BF16, side="right")] * 2
            wbuf_f = sb(p1_es, f"wbf{b}", [128, 1024], F32, side="right")
            wbuf_b = [sb(p1_es, f"wbb{b}_{p}", [128, 2048], BF16, side="right")
                      for p in range(2)]

            xhat_base = S.now("s_xhat")
            xT_base = S.now("s_xT")

            _tt_lim = TT
            if stop and stop.startswith("p1a:"):
                _tt_lim = int(stop.split(":")[1])
            for tt in range(_tt_lim):
                par = tt % 2
                ln = lns[par]
                w = [("s_xhat", xhat_base + tt - 1),
                     ("s_out0", S.now("s_out0")), ("s_out1", S.now("s_out1"))]
                if tt < 2:
                    w.append(("s_x2", S.now("s_x2")))
                S.dma("sync", xt_in[par].ap(),
                      d_x[b, 128 * tt:128 * tt + 128, :], wait=w,
                      inc=f"s_x{par}")
                nx = S.now(f"s_x{par}")
                sx = f"s_x{par}"
                S.add("scalar",
                      lambda e, par=par, ln=ln: e.activation(
                          out=sq_junk.ap(), in_=xt_in[par].ap(), func=AF.Square,
                          accum_out=ln.ap()[:, 1:2]),
                      wait=[(sx, nx), ("s_trig", n_trig),
                            ("s_lnv", S.now("s_lnv") - 1)],
                      inc="s_stats")
                S.add("vector",
                      lambda e, par=par, ln=ln: e.reduce_sum(
                          ln.ap()[:, 0:1], xt_in[par].ap(), AX.X),
                      wait=[(sx, nx), ("s_lnr", S.now("s_lnr") - 1)],
                      inc="s_stats")

                emit_ln_smalls(S, ln)
                S.add("scalar",
                      lambda e, ln=ln: e.activation(
                          out=ln.ap()[:, 6:7], in_=ln.ap()[:, 5:6], func=AF.Ln),
                      wait=[("s_lnv", S.now("s_lnv"))], inc="s_chs")
                S.add("scalar",
                      lambda e, ln=ln: e.activation(
                          out=ln.ap()[:, 7:8], in_=ln.ap()[:, 6:7], func=AF.Exp,
                          scale=-0.5),
                      wait=[("s_chs", S.now("s_chs"))], inc="s_lnr")
                S.add("vector",
                      lambda e, ln=ln: e.tensor_mul(
                          ln.ap()[:, 9:10], ln.ap()[:, 8:9], ln.ap()[:, 7:8]),
                      wait=[("s_lnr", S.now("s_lnr")),
                            ("s_xhat", S.now("s_xhat") - 1)], inc="s_lnm")
                S.add("scalar",
                      lambda e, par=par, ln=ln: e.activation(
                          out=xh1[par].ap(), in_=xt_in[par].ap(),
                          func=AF.Identity,
                          bias=ln.ap()[:, 9:10], scale=ln.ap()[:, 7:8]),
                      wait=[("s_lnm", S.now("s_lnm")),
                            ("s_xTmm", S.now("s_xTmm"))],
                      inc="s_xhat")
                nxh = S.now("s_xhat")
                for half in range(2):
                    bank = pb[6 + half].ap().bitcast(BF16)
                    for j in range(4):
                        dc = 4 * half + j
                        w = []
                        if j == 0:
                            w = [("s_xhat", nxh),
                                 ("s_xT", xT_base + 2 * tt + half - 1)]
                            if tt == 0:
                                w += [("s_wbv", setup_done["wbv"]),
                                      ("s_bvbe", setup_done["bvbe"]),
                                      ("s_ptev", S.now("s_ptev"))]
                        S.add("tensor",
                              lambda e, par=par, dc=dc, bank=bank, j=j:
                              e.transpose(
                                  bank[0:128, 128 * j:128 * j + 128],
                                  xh1[par].ap()[:, 128 * dc:128 * dc + 128],
                                  ident.ap()),
                              wait=w, inc="s_xTmm")
                    S.add("vector",
                          lambda e, tt=tt, half=half, bank=bank, xhatT=xhatT:
                          e.tensor_copy(
                              bass.AP(xhatT, 4 * T * half + 128 * tt,
                                      [[8 * T, 128], [T, 4], [1, 128]]),
                              bank[0:128, 0:512]),
                          wait=[("s_xTmm", S.now("s_xTmm"))], inc="s_xT")
            nxt_done = S.now("s_xT")
            if stop and stop.startswith("p1a"):
                p1_es.close()
                r1_es.close()
                bat_es.close()
                return out_total

            rope_base = S.now("s_rope")
            vev_base = S.now("s_vev")
            qkdst = {0: ("QE", "QO"), 1: ("KE", "KO")}
            blocks = {"QE": QE, "QO": QO, "KE": KE, "KO": KO}
            for wch in range(12):
                wpar = wch % 2
                for wh in range(2):
                    wload = [("s_wcast", S.now("s_wcast"))]
                    if b == 0 and wch == 0 and wh == 0:
                        wload += [("s_wbmm", nwb_mm), ("s_wbv", nwb_v)]
                    S.dma("sync",
                          wbuf_f.ap().rearrange("p (c f) -> p c f", f=128),
                          d_wqkvT.rearrange("(c p) f -> p c f", p=128)[
                              :, :, 256 * wch + 128 * wh:
                              256 * wch + 128 * wh + 128],
                          wait=wload, inc="s_wld")
                    gate = []
                    if wch >= 2:
                        gate = [("s_qk", S.now("s_qk")),
                                ("s_vmm", S.now("s_vmm"))]
                    S.add("vector",
                          lambda e, wpar=wpar, wh=wh: e.tensor_copy(
                              bass.AP(wbuf_b[wpar], 128 * wh,
                                      [[2048, 128], [256, 8], [1, 128]]),
                              wbuf_f.ap()),
                          wait=[("s_wld", S.now("s_wld"))] + gate,
                          inc="s_wcast")
                ncast = S.now("s_wcast")
                if wch < 8:
                    pair = wch
                    dE, dO = qkdst[pair // 4]
                    jb = pair % 4
                    bcE = bcols.ap()[:, 2 * pair:2 * pair + 1]
                    bcO = bcols.ap()[:, 2 * pair + 1:2 * pair + 2]
                    for sp in range(NSP):
                        ppar = sp % 2
                        for eo in range(2):
                            for dck in range(8):
                                w = []
                                if eo == 0 and dck == 0:
                                    w = [("s_wcast", ncast), ("s_xT", nxt_done),
                                         ("s_rope", S.now("s_rope") - 1),
                                         ("s_x2", S.now("s_x2")),
                                         ("s_fT", S.now("s_fT"))]
                                S.add("tensor",
                                      lambda e, ppar=ppar, eo=eo, dck=dck, sp=sp,
                                      wpar=wpar: e.matmul(
                                          pb[2 * ppar + eo][0:128, 0:SPW],
                                          wbuf_b[wpar].ap()[
                                              :, 256 * dck + 128 * eo:
                                              256 * dck + 128 * eo + 128],
                                          xhatT.ap()[:, T * dck + SPW * sp:
                                                     T * dck + SPW * sp + SPW],
                                          start=(dck == 0), stop=(dck == 7)),
                                      wait=w,
                                      inc="s_qk" if (eo == 1 and dck == 7) else None)
                        nqk = S.now("s_qk")
                        eout = blocks[dE].ap()[:, T * jb + SPW * sp:
                                               T * jb + SPW * sp + SPW]
                        oout = blocks[dO].ap()[:, T * jb + SPW * sp:
                                               T * jb + SPW * sp + SPW]
                        csl = c4.ap()[:, SPW * sp:SPW * sp + SPW]
                        ssl = s4.ap()[:, SPW * sp:SPW * sp + SPW]

                        wr = [("s_qk", nqk), ("s_rope", S.now("s_rope"))]
                        if b == 0 and pair == 0 and sp == 0:
                            wr.append(("s_ld", nld))
                        pE = pb[2 * ppar][0:128, 0:SPW]
                        pO = pb[2 * ppar + 1][0:128, 0:SPW]
                        def rope_a(e, pE=pE, pO=pO, csl=csl, ssl=ssl, bcE=bcE,
                                   bcO=bcO):
                            e.scalar_tensor_tensor(rs[0].ap(), pE, bcE, csl,
                                                   ALU.add, ALU.mult)
                            return e.scalar_tensor_tensor(rs[1].ap(), pO, bcO,
                                                          ssl, ALU.add, ALU.mult)
                        S.add("vector", rope_a, wait=wr, inc="s_chv")
                        S.add("vector",
                              lambda e, eout=eout: e.tensor_sub(
                                  eout, rs[0].ap(), rs[1].ap()),
                              wait=[("s_chv", S.now("s_chv"))], inc="s_chv")
                        def rope_b(e, pE=pE, pO=pO, csl=csl, ssl=ssl, bcE=bcE,
                                   bcO=bcO):
                            e.scalar_tensor_tensor(rs[0].ap(), pE, bcE, ssl,
                                                   ALU.add, ALU.mult)
                            return e.scalar_tensor_tensor(rs[1].ap(), pO, bcO,
                                                          csl, ALU.add, ALU.mult)
                        S.add("vector", rope_b,
                              wait=[("s_chv", S.now("s_chv"))], inc="s_chv")
                        S.add("vector",
                              lambda e, oout=oout: e.tensor_add(
                                  oout, rs[0].ap(), rs[1].ap()),
                              wait=[("s_chv", S.now("s_chv"))], inc="s_rope")
                else:
                    vsp = wch - 8
                    col = 256 * (vsp % 2)
                    for tt in range(TT):
                        vpar = tt % 2
                        for dck in range(8):
                            w = []
                            if dck == 0:
                                w = [("s_wcast", ncast), ("s_xT", nxt_done),
                                     ("s_vev", S.now("s_vev") - 1),
                                     ("s_x3", S.now("s_x3"))]
                            S.add("tensor",
                                  lambda e, vpar=vpar, dck=dck, tt=tt, col=col,
                                  wpar=wpar: e.matmul(
                                      pb[4 + vpar][0:128, col:col + 256],
                                      xhatT.ap()[:, T * dck + 128 * tt:
                                                 T * dck + 128 * tt + 128],
                                      wbuf_b[wpar].ap()[:, 256 * dck:256 * dck + 256],
                                      start=(dck == 0), stop=(dck == 7)),
                                  wait=w, inc="s_vmm" if dck == 7 else None)
                        S.add("vector",
                              lambda e, vpar=vpar, tt=tt, vsp=vsp, col=col,
                              Vsb=Vsb: e.tensor_add(
                                  Vsb.ap().rearrange("p (c h w) -> p c h w",
                                                     c=TT, h=16)[
                                      :, tt:tt + 1, 4 * vsp:4 * vsp + 4, 0:64],
                                  pb[4 + vpar][0:128, col:col + 256].rearrange(
                                      "p (o h w) -> p o h w", o=1, h=4),
                                  bvb.ap()[:, 256 * vsp:256 * vsp + 256].rearrange(
                                      "p (o h w) -> p o h w", o=1, h=4)),
                              wait=[("s_vmm", S.now("s_vmm"))], inc="s_vev")
            n_rope = S.now("s_rope")
            n_vmm = S.now("s_vmm")
            n_vev = S.now("s_vev")
            p1_es.close()
            r1_es.close()
            if stop == "p1b":
                bat_es.close()
                return out_total

            p23_es = contextlib.ExitStack()
            On = sb(p23_es, f"On{b}", [128, 8 * T], BF16, side="right")
            woutT_b = sb(p23_es, f"woT{b}", [128, 8 * D], BF16, side="right")
            wo_f = sb(p23_es, f"woF{b}", [128, 2048], F32, side="right")
            ptil = [sb(p23_es, f"pt{b}_{p}", [128, T], BF16, side="right")
                    for p in range(2)]
            ptT = [sb(p23_es, f"ptT{b}_{p}", [128, T], BF16, side="right")
                   for p in range(2)]
            mxl = [sb(p23_es, f"mxl{b}_{p}", [128, 32], F32, side="right")
                   for p in range(2)]
            ostg = [sb(p23_es, f"ost{b}_{i}", [64, 128], BF16, side="right")
                    for i in range(2)]
            scd = [sb(p23_es, f"scd{b}_{p}", [128, 128], F32, side="right")
                   for p in range(2)]

            for j in range(4):
                S.dma("sync", wo_f.ap().rearrange("p (c f) -> p c f", f=1024),
                      d_woutT.rearrange("(c p) f -> p c f", p=128)[:, 2 * j:2 * j + 2, :],
                      wait=[("s_vmm", n_vmm), ("s_wocast", S.now("s_wocast"))],
                      inc="s_wold")
                S.add("vector",
                      lambda e, j=j: e.tensor_copy(
                          woutT_b.ap()[:, 2048 * j:2048 * j + 2048], wo_f.ap()),
                      wait=[("s_wold", S.now("s_wold"))], inc="s_wocast")
            n_wocast = S.now("s_wocast")

            units = [(tt, hh) for tt in range(TT) for hh in range(16)]
            NU = len(units)
            rec = [dict() for _ in range(NU)]

            def spans_of(tt):
                nk = 128 * (tt + 1)
                ns = (nk + SPW - 1) // SPW
                return [min(SPW, nk - SPW * s) for s in range(ns)]

            def stage_a(i):
                tt, hh = units[i]
                jb, hm = hh // 4, hh % 4
                par = i % 2
                widths = spans_of(tt)
                ns = len(widths)
                b0 = p2st["sc_ctr"]
                p2st["sc_ctr"] += ns
                r = rec[i]
                r["par"], r["ns"], r["widths"], r["b0"] = par, ns, widths, b0
                sc0 = S.now("s_sc")
                for s_ in range(ns):
                    w_ = widths[s_]
                    bk = (b0 + s_) % 4
                    wt = list(p2st["scb"][bk])
                    if i == 0 and s_ == 0:
                        wt += [("s_rope", n_rope)]
                    for eo, (Qb, Kb) in enumerate(((QE, KE), (QO, KO))):
                        S.add("tensor",
                              lambda e, bk=bk, w_=w_, s_=s_, hm=hm, jb=jb,
                              tt=tt, Qb=Qb, Kb=Kb, eo=eo: e.matmul(
                                  pb[bk][0:128, 0:w_],
                                  Qb.ap()[32 * hm:32 * hm + 32,
                                          T * jb + 128 * tt:
                                          T * jb + 128 * tt + 128],
                                  Kb.ap()[32 * hm:32 * hm + 32,
                                          T * jb + SPW * s_:
                                          T * jb + SPW * s_ + w_],
                                  start=(eo == 0), stop=(eo == 1),
                                  tile_position=(32 * hm, 0)),
                              wait=wt if eo == 0 else (),
                              inc="s_sc" if eo == 1 else None)
                r["sc"] = S.now("s_sc")
                ml = mxl[par]
                sc_ = scd[par]
                lb = (b0 + ns - 1) % 4
                doff = widths[-1] - 128
                wg = []
                if i >= 2:
                    wg.append(("s_fct", rec[i - 2]["fct"]))
                if b == 0 and i == 0:
                    wg.append(("s_ld", nld))
                mx0 = S.now("s_mx")
                mi = 0
                for s_ in range(ns - 1):
                    S.add("vector",
                          lambda e, ml=ml, mi=mi, bk=(b0 + s_) % 4,
                          w_=widths[s_]: e.reduce_max(
                              ml.ap()[:, mi:mi + 1], pb[bk][0:128, 0:w_],
                              AX.X),
                          wait=wg + [("s_sc", sc0 + s_ + 1)] if (wg or True)
                          else (), inc="s_chv")
                    wg = []
                    S.add("vector",
                          lambda e, ml=ml, mi=mi: e.tensor_scalar(
                              ml.ap()[:, 6 + mi:7 + mi], ml.ap()[:, mi:mi + 1],
                              -0.125, None, ALU.mult),
                          wait=[("s_chv", S.now("s_chv"))], inc="s_mx")
                    mi += 1
                cpc = S.add("vector",
                            lambda e, lb=lb, doff=doff, sc_=sc_:
                            e.scalar_tensor_tensor(
                                sc_.ap(), pb[lb][0:128, doff:doff + 128],
                                1.0, negmask.ap(), ALU.mult, ALU.add),
                            wait=wg + [("s_sc", sc0 + ns)], inc="s_chv")
                r["cpc"] = cpc
                if doff > 0:
                    S.add("vector",
                          lambda e, ml=ml, mi=mi, lb=lb, doff=doff:
                          e.reduce_max(ml.ap()[:, mi:mi + 1],
                                       pb[lb][0:128, 0:doff], AX.X),
                          inc="s_chv")
                    S.add("vector",
                          lambda e, ml=ml, mi=mi: e.tensor_scalar(
                              ml.ap()[:, 6 + mi:7 + mi], ml.ap()[:, mi:mi + 1],
                              -0.125, None, ALU.mult),
                          wait=[("s_chv", S.now("s_chv"))], inc="s_mx")
                    mi += 1
                S.add("vector",
                      lambda e, ml=ml, mi=mi, sc_=sc_: e.reduce_max(
                          ml.ap()[:, mi:mi + 1], sc_.ap(), AX.X),
                      wait=[("s_chv", cpc)], inc="s_chv")
                S.add("vector",
                      lambda e, ml=ml, mi=mi: e.tensor_scalar(
                          ml.ap()[:, 6 + mi:7 + mi], ml.ap()[:, mi:mi + 1],
                          -0.125, None, ALU.mult),
                      wait=[("s_chv", S.now("s_chv"))], inc="s_mx")
                mi += 1
                nmx = mi
                r["nmx"] = nmx
                S.add("vector",
                      lambda e, ml=ml, nmx=nmx: e.reduce_max(
                          ml.ap()[:, 24:25], ml.ap()[:, 0:nmx], AX.X),
                      wait=[("s_chv", S.now("s_chv"))], inc="s_chv")
                S.add("vector",
                      lambda e, ml=ml: e.tensor_scalar(
                          ml.ap()[:, 25:26], ml.ap()[:, 24:25], -0.125,
                          None, ALU.mult),
                      wait=[("s_chv", S.now("s_chv"))], inc="s_mx")
                r["mxM"] = S.now("s_mx")
                wt0 = []
                if i >= 2:
                    wt0 += [("s_psc", rec[i - 2]["psc"]),
                            ("s_ptT", rec[i - 2]["ptT"])]
                mi = 0
                for s_ in range(ns - 1):
                    bk = (b0 + s_) % 4
                    cnt = S.add("scalar",
                                lambda e, ml=ml, s_=s_, bk=bk, w_=widths[s_],
                                par=par, mi=mi: e.activation(
                                    out=ptil[par].ap()[:, SPW * s_:
                                                       SPW * s_ + w_],
                                    in_=pb[bk][0:128, 0:w_], func=AF.Exp,
                                    scale=0.125,
                                    bias=ml.ap()[:, 6 + mi:7 + mi],
                                    accum_out=ml.ap()[:, 12 + mi:13 + mi]),
                                wait=(wt0 if mi == 0 else [])
                                + [("s_mx", mx0 + mi + 1)], inc="s_exp")
                    p2st["scb"][bk] = [("s_exp", cnt)]
                    mi += 1
                dbase = SPW * (ns - 1)
                lbw = [("s_chv", cpc)]
                if doff > 0:
                    cnt = S.add("scalar",
                                lambda e, ml=ml, lb=lb, doff=doff, par=par,
                                mi=mi, dbase=dbase: e.activation(
                                    out=ptil[par].ap()[:, dbase:dbase + doff],
                                    in_=pb[lb][0:128, 0:doff], func=AF.Exp,
                                    scale=0.125,
                                    bias=ml.ap()[:, 6 + mi:7 + mi],
                                    accum_out=ml.ap()[:, 12 + mi:13 + mi]),
                                wait=(wt0 if mi == 0 else [])
                                + [("s_mx", mx0 + mi + 1)], inc="s_exp")
                    lbw.append(("s_exp", cnt))
                    mi += 1
                S.add("scalar",
                      lambda e, ml=ml, sc_=sc_, par=par, mi=mi,
                      dbase=dbase, doff=doff: e.activation(
                          out=ptil[par].ap()[:, dbase + doff:
                                             dbase + doff + 128],
                          in_=sc_.ap(), func=AF.Exp, scale=0.125,
                          bias=ml.ap()[:, 6 + mi:7 + mi],
                          accum_out=ml.ap()[:, 12 + mi:13 + mi]),
                      wait=(wt0 if mi == 0 else [])
                      + [("s_mx", mx0 + mi + 1)], inc="s_exp")
                p2st["scb"][lb] = lbw
                r["exp"] = S.now("s_exp")
                S.add("scalar",
                      lambda e, ml=ml, nmx=nmx: e.activation(
                          out=ml.ap()[:, 18:18 + nmx], in_=ml.ap()[:, 0:nmx],
                          func=AF.Exp, scale=0.125, bias=ml.ap()[:, 25:26]),
                      wait=[("s_mx", r["mxM"])], inc="s_fct")
                r["fct"] = S.now("s_fct")

            def stage_b_dve(i):
                tt, hh = units[i]
                par, ns, widths = rec[i]["par"], rec[i]["ns"], rec[i]["widths"]
                nmx = rec[i]["nmx"]
                doff = widths[-1] - 128
                ml = mxl[par]
                wls = [("s_fct", rec[i]["fct"])]
                if i >= 2:
                    wls.append(("s_psc", rec[i - 2]["psc"]))
                S.add("vector",
                      lambda e, ml=ml, nmx=nmx: e.tensor_mul(
                          ml.ap()[:, 12:12 + nmx], ml.ap()[:, 12:12 + nmx],
                          ml.ap()[:, 18:18 + nmx]),
                      wait=wls, inc="s_chv")
                S.add("vector",
                      lambda e, ml=ml, nmx=nmx: e.reduce_sum(
                          ml.ap()[:, 26:27], ml.ap()[:, 12:12 + nmx], AX.X),
                      wait=[("s_chv", S.now("s_chv"))], inc="s_chv")
                S.add("vector",
                      lambda e, ml=ml: e.reciprocal(ml.ap()[:, 27:28],
                                                    ml.ap()[:, 26:27]),
                      wait=[("s_chv", S.now("s_chv"))], inc="s_chv")
                S.add("vector",
                      lambda e, ml=ml, nmx=nmx: e.tensor_scalar(
                          ml.ap()[:, 18:18 + nmx], ml.ap()[:, 18:18 + nmx],
                          ml.ap()[:, 27:28], None, ALU.mult),
                      wait=[("s_chv", S.now("s_chv"))], inc="s_lrec")
                rec[i]["lrec"] = S.now("s_lrec")
                psc0 = S.now("s_psc")
                parts = []
                for s_ in range(ns - 1):
                    parts.append((SPW * s_, widths[s_]))
                dbase = SPW * (ns - 1)
                if doff > 0:
                    parts.append((dbase, doff))
                parts.append((dbase + doff, 128))
                for mi, (po, pw) in enumerate(parts):
                    wt = [("s_lrec", rec[i]["lrec"])] if mi == 0 else []
                    S.add("gpsimd",
                          lambda e, ml=ml, mi=mi, po=po, pw=pw, par=par:
                          e.tensor_scalar(
                              ptil[par].ap()[:, po:po + pw],
                              ptil[par].ap()[:, po:po + pw],
                              ml.ap()[:, 18 + mi:19 + mi], None, ALU.mult),
                          wait=wt, inc="s_psc")
                rec[i]["psc"] = S.now("s_psc")
                rec[i]["psc_span"] = [psc0 + s_ + 1 for s_ in range(ns - 1)] \
                    + [psc0 + len(parts)]

            def stage_b(i):
                par, ns, widths = rec[i]["par"], rec[i]["ns"], rec[i]["widths"]
                psc_span = rec[i]["psc_span"]
                ev0 = S.now("s_ptev")
                rec[i]["ev0"] = ev0
                for s_ in range(ns):
                    w_ = widths[s_]
                    k_ = p2st["stg_ctr"]
                    stg = k_ % 2
                    p2st["stg_ctr"] += 1
                    bank = pb[6 + stg].ap().bitcast(BF16)
                    for j in range(w_ // 128):
                        wt = []
                        if j == 0:
                            wt = [("s_psc", psc_span[s_]),
                                  ("s_ptev", k_ - 1)]
                            if i == 0:
                                wt.append(("s_xT", nxt_done))
                        S.add("tensor",
                              lambda e, bank=bank, j=j, s_=s_, par=par:
                              e.transpose(
                                  bank[0:128, 128 * j:128 * j + 128],
                                  ptil[par].ap()[:, SPW * s_ + 128 * j:
                                                 SPW * s_ + 128 * j + 128],
                                  ident.ap()),
                              wait=wt,
                              inc="s_ptT" if j == w_ // 128 - 1 else None)
                    wt = [("s_ptT", S.now("s_ptT"))]
                    if s_ == 0 and i >= 2:
                        wt.append(("s_pvm", rec[i - 2]["pvm"]))
                    S.add("scalar",
                          lambda e, bank=bank, s_=s_, w_=w_, par=par:
                          e.activation(
                              out=ptT[par].ap()[:, SPW * s_:SPW * s_ + w_],
                              in_=bank[0:128, 0:w_], func=AF.Copy),
                          wait=wt, inc="s_ptev")
                rec[i]["ptT"] = S.now("s_ptT")

            def stage_c(i):
                tt, hh = units[i]
                par, ns, widths = rec[i]["par"], rec[i]["ns"], rec[i]["widths"]
                ev0 = rec[i]["ev0"]
                pvb = 4 + (i % 2)
                for c in range(tt + 1):
                    wt = []
                    if c % 4 == 0:
                        wt = [("s_ptev", ev0 + c // 4 + 1)]
                        if c == 0:
                            if i >= 2:
                                wt.append(("s_onev", rec[i - 2]["onev"]))
                            else:
                                wt.append(("s_vev", n_vev))
                    S.add("tensor",
                          lambda e, c=c, hh=hh, par=par, pvb=pvb, tt=tt:
                          e.matmul(
                              pb[pvb][0:64, 0:128],
                              Vsb.ap()[:, 1024 * c + 64 * hh:
                                       1024 * c + 64 * hh + 64],
                              ptT[par].ap()[:, 128 * c:128 * c + 128],
                              start=(c == 0), stop=(c == tt)),
                          wait=wt, inc="s_pvm" if c == tt else None)
                rec[i]["pvm"] = S.now("s_pvm")

            def stage_d(i):
                tt, hh = units[i]
                pvb = 4 + (i % 2)
                dck = hh // 2
                if hh % 2 == 0:
                    S.add("vector",
                          lambda e, pvb=pvb, dck=dck, tt=tt: e.tensor_copy(
                              On.ap()[0:64, T * dck + 128 * tt:
                                      T * dck + 128 * tt + 128],
                              pb[pvb][0:64, 0:128]),
                          wait=[("s_pvm", rec[i]["pvm"])], inc="s_onev")
                else:
                    o = p2st["odd_ctr"] % 2
                    p2st["odd_ctr"] += 1
                    S.add("vector",
                          lambda e, pvb=pvb, o=o: e.tensor_copy(
                              ostg[o].ap(), pb[pvb][0:64, 0:128]),
                          wait=[("s_pvm", rec[i]["pvm"]),
                                (f"s_onmv{o}", p2st["oddb"][o])],
                          inc="s_onev")
                    nv = S.now("s_onev")
                    p2st["oddb"][o] = S.dma(
                        "gpsimd",
                        On.ap()[64:128, T * dck + 128 * tt:
                                T * dck + 128 * tt + 128],
                        ostg[o].ap(), wait=[("s_onev", nv)],
                        inc=f"s_onmv{o}")
                rec[i]["onev"] = S.now("s_onev")

            for g in range(NU + 3):
                if 1 <= g < NU + 1:
                    stage_b_dve(g - 1)
                if g < NU:
                    stage_a(g)
                if 2 <= g < NU + 2:
                    stage_c(g - 2)
                if 1 <= g < NU + 1:
                    stage_b(g - 1)
                if 3 <= g:
                    stage_d(g - 3)
            n_onev = S.now("s_onev")
            n_onmv0 = S.now("s_onmv0")
            n_onmv1 = S.now("s_onmv1")
            if dbg and b == 0:
                wdbg = [("s_onev", n_onev), ("s_onmv0", n_onmv0),
                        ("s_onmv1", n_onmv1)]
                S.dma("scalar", d_dbg["dOn"], On.ap(), wait=wdbg, inc="s_dbg")
                S.dma("scalar", d_dbg["dptil"], ptil[1].ap(), wait=wdbg,
                      inc="s_dbg")
                S.dma("scalar", d_dbg["dptT"], ptT[1].ap(), wait=wdbg,
                      inc="s_dbg")
                S.dma("scalar", d_dbg["dmask"], negmask.ap(), wait=wdbg,
                      inc="s_dbg")
            bat_es.close()
            if stop == "p2":
                p23_es.close()
                return out_total

            p3_es = contextlib.ExitStack()
            x2b = [sb(p3_es, f"x2b{b}_{p}", [128, D], F32) for p in range(2)]
            xh2 = [sb(p3_es, f"xh2{b}_{p}", [128, D], BF16) for p in range(2)]
            xh2T = [sb(p3_es, f"xh2T{b}_{p}", [128, D], BF16) for p in range(2)]
            x3b = [sb(p3_es, f"x3b{b}_{p}", [128, D], F32) for p in range(2)]

            for tt in range(TT):
                par = tt % 2
                ln = lns[par]
                w = [("s_x2", S.now("s_x2") - 2), ("s_xhat", S.now("s_xhat")),
                     ("s_pvm", S.now("s_pvm"))]
                S.dma("sync", xt_in[par].ap(),
                      d_x[b, 128 * tt:128 * tt + 128, :], wait=w,
                      inc=f"s_x{par}")
                nx = S.now(f"s_x{par}")
                sx = f"s_x{par}"
                for fsp in range(2):
                    for dck in range(8):
                        w = []
                        if dck == 0:
                            w = [("s_onmv0", n_onmv0), ("s_onmv1", n_onmv1),
                                 ("s_onev", n_onev),
                                 ("s_wocast", n_wocast),
                                 ("s_x2", S.now("s_x2"))]
                        S.add("tensor",
                              lambda e, fsp=fsp, dck=dck, tt=tt, On=On:
                              e.matmul(
                                  pb[fsp][0:128, 0:512],
                                  On.ap()[:, T * dck + 128 * tt:
                                          T * dck + 128 * tt + 128],
                                  woutT_b.ap()[:, D * dck + 512 * fsp:
                                               D * dck + 512 * fsp + 512],
                                  start=(dck == 0), stop=False),
                              wait=w, inc=None)
                    S.add("tensor",
                          lambda e, fsp=fsp: e.matmul(
                              pb[fsp][0:128, 0:512], ones_bf.ap()[0:1, 0:128],
                              boutrow.ap()[0:1, 512 * fsp:512 * fsp + 512],
                              start=False, stop=True),
                          inc="s_op")
                nop = S.now("s_op")
                for fsp in range(2):
                    S.add("vector",
                          lambda e, fsp=fsp, par=par: e.tensor_add(
                              x2b[par].ap()[:, 512 * fsp:512 * fsp + 512],
                              xt_in[par].ap()[:, 512 * fsp:512 * fsp + 512],
                              pb[fsp][0:128, 0:512]),
                          wait=[("s_op", nop), (sx, nx),
                                ("s_xh2", S.now("s_xh2") - 1)], inc="s_x2")
                nx2 = S.now("s_x2")
                S.add("scalar",
                      lambda e, par=par, ln=ln: e.activation(
                          out=sq_junk.ap(), in_=x2b[par].ap(), func=AF.Square,
                          accum_out=ln.ap()[:, 1:2]),
                      wait=[("s_x2", nx2), ("s_lnv", S.now("s_lnv") - 1)],
                      inc="s_stats")
                S.add("vector",
                      lambda e, par=par, ln=ln: e.reduce_sum(
                          ln.ap()[:, 0:1], x2b[par].ap(), AX.X),
                      wait=[("s_x2", nx2), ("s_lnr", S.now("s_lnr") - 1)],
                      inc="s_stats")
                emit_ln_smalls(S, ln)
                S.add("scalar",
                      lambda e, ln=ln: e.activation(
                          out=ln.ap()[:, 6:7], in_=ln.ap()[:, 5:6], func=AF.Ln),
                      wait=[("s_lnv", S.now("s_lnv"))], inc="s_chs")
                S.add("scalar",
                      lambda e, ln=ln: e.activation(
                          out=ln.ap()[:, 7:8], in_=ln.ap()[:, 6:7], func=AF.Exp,
                          scale=-0.5),
                      wait=[("s_chs", S.now("s_chs"))], inc="s_lnr")
                S.add("vector",
                      lambda e, ln=ln: e.tensor_mul(
                          ln.ap()[:, 9:10], ln.ap()[:, 8:9], ln.ap()[:, 7:8]),
                      wait=[("s_lnr", S.now("s_lnr")),
                            ("s_xh2", S.now("s_xh2") - 1)], inc="s_lnm")
                S.add("scalar",
                      lambda e, par=par, ln=ln: e.activation(
                          out=xh2[par].ap(), in_=x2b[par].ap(),
                          func=AF.Identity,
                          bias=ln.ap()[:, 9:10], scale=ln.ap()[:, 7:8]),
                      wait=[("s_lnm", S.now("s_lnm")),
                            ("s_fTmm", S.now("s_fTmm") - 8)],
                      inc="s_xh2")
                nxh2 = S.now("s_xh2")
                for half2 in range(2):
                    bank = pb[2 + half2].ap().bitcast(BF16)
                    for j in range(4):
                        dc = 4 * half2 + j
                        wtr = []
                        if j == 0:
                            wtr = [("s_xh2", nxh2),
                                   ("s_fT", S.now("s_fT") - 1)]
                        S.add("tensor",
                              lambda e, par=par, dc=dc, bank=bank, j=j:
                              e.transpose(bank[0:128, 128 * j:128 * j + 128],
                                          xh2[par].ap()[:, 128 * dc:128 * dc + 128],
                                          ident.ap()),
                              wait=wtr, inc="s_fTmm")
                    S.add("vector",
                          lambda e, par=par, half2=half2, bank=bank:
                          e.tensor_copy(
                              xh2T[par].ap()[:, 512 * half2:512 * half2 + 512],
                              bank[0:128, 0:512]),
                          wait=[("s_fTmm", S.now("s_fTmm")),
                                ("s_ffn", S.now("s_ffn") - 2)], inc="s_fT")
                nfT = S.now("s_fT")
                for half in range(2):
                    for ic in range(8):
                        if half == 0:
                            wmat = wt_r if ic < 4 else wt_ni
                        else:
                            wmat = wt_i if ic < 4 else wt_r
                        wcol = 512 * (ic % 4)
                        w = []
                        if ic == 0:
                            w = [("s_fT", nfT), ("s_cst", setup_done["cst"]),
                                 ("s_x3", S.now("s_x3"))]
                        S.add("tensor",
                              lambda e, half=half, ic=ic, par=par, wmat=wmat,
                              wcol=wcol: e.matmul(
                                  pb[4 + half][0:128, 0:512],
                                  xh2T[par].ap()[:, 128 * ic:128 * ic + 128],
                                  wmat.ap()[:, wcol:wcol + 512],
                                  start=(ic == 0), stop=False),
                              wait=w, inc=None)
                    S.add("tensor",
                          lambda e, half=half: e.matmul(
                              pb[4 + half][0:128, 0:512],
                              ones_bf.ap()[0:1, 0:128],
                              fbrow.ap()[0:1, 512 * half:512 * half + 512],
                              start=False, stop=True),
                          inc="s_ffn")
                nffn = S.now("s_ffn")
                for half in range(2):
                    S.add("vector",
                          lambda e, half=half, par=par: e.tensor_add(
                              x3b[par].ap()[:, 512 * half:512 * half + 512],
                              x2b[par].ap()[:, 512 * half:512 * half + 512],
                              pb[4 + half][0:128, 0:512]),
                          wait=[("s_ffn", nffn),
                                (f"s_out{par}", S.now(f"s_out{par}"))],
                          inc="s_x3")
                S.dma("gpsimd", d_y[b, 128 * tt:128 * tt + 128, :],
                      x3b[par].ap(), wait=[("s_x3", S.now("s_x3"))],
                      inc=f"s_out{par}")
            out_total += TT
            p3_es.close()
            p23_es.close()
            return out_total

        p2st = {"sc_ctr": 0, "stg_ctr": 0, "odd_ctr": 0,
                "scb": [[], [], [], []], "oddb": [0, 0]}

        out_total = 0
        if stop != "setup":
            for b in range(NB):
                out_total = emit_batch(b, out_total)

        f0, f1 = S.now("s_out0"), S.now("s_out1")
        fd = S.now("s_dbg")
        S.q["gpsimd"].append(
            (lambda e: (e.wait_ge(S.sem("s_out0"), f0),
                        e.wait_ge(S.sem("s_dbg"), fd) if fd else None,
                        e.wait_ge(S.sem("s_out1"), f1))[-1], [], None))

        block = es.enter_context(nc.Block())
        S.emit(block)
    return nc


def emit_ln_smalls(S, ln):
    a = ln.ap()
    S.add("vector", lambda e: e.tensor_scalar_mul(a[:, 2:3], a[:, 0:1], 1.0 / D),
          wait=[("s_stats", S.now("s_stats"))], inc="s_chv")
    S.add("vector", lambda e: e.tensor_scalar(a[:, 4:5], a[:, 1:2], 1.0 / D,
                                              None, ALU.mult))
    S.add("vector", lambda e: e.tensor_mul(a[:, 3:4], a[:, 2:3], a[:, 2:3]),
          wait=[("s_chv", S.now("s_chv"))], inc="s_chv")
    S.add("vector", lambda e: e.tensor_scalar_mul(a[:, 8:9], a[:, 2:3], -1.0))
    S.add("vector", lambda e: e.tensor_scalar(a[:, 5:6], a[:, 4:5], a[:, 3:4],
                                              EPS, ALU.subtract, ALU.add),
          wait=[("s_chv", S.now("s_chv"))], inc="s_lnv")



def _prep_weights(w_qkv, b_qkv, w_out, G1_R, G2_R, G1_I, G2_I):
    perm = _qk_perm()
    wp = np.ascontiguousarray(np.asarray(w_qkv)[perm])
    bp = np.ascontiguousarray(np.asarray(b_qkv)[perm])
    wqkvT = np.ascontiguousarray(wp.T)
    bcols = np.ascontiguousarray(bp[:2048].reshape(16, 128).T)
    brow = bp.reshape(1, 3 * D)
    woutT = np.ascontiguousarray(np.asarray(w_out).T)
    g1 = {"r": np.ascontiguousarray(np.asarray(G1_R)[0].transpose(1, 2, 0)),
          "i": np.ascontiguousarray(np.asarray(G1_I)[0].transpose(1, 2, 0))}
    g2 = {"r": np.ascontiguousarray(np.asarray(G2_R)[:, :, 0, :].transpose(0, 2, 1)),
          "i": np.ascontiguousarray(np.asarray(G2_I)[:, :, 0, :].transpose(0, 2, 1))}
    return wqkvT, bcols, brow, woutT, g1, g2


def kernel(L0_attn_x, L0_attn_w_qkv, L0_attn_b_qkv, L0_attn_w_out, L0_attn_b_out,
           L0_ffn_G1_R, L0_ffn_G2_R, L0_ffn_G1_I, L0_ffn_G2_I,
           L0_ffn_bias_real, L0_ffn_bias_imag):
    x = np.ascontiguousarray(np.asarray(L0_attn_x, dtype=np.float32))
    B, T, _ = x.shape
    NB = B // NCORES
    wqkvT, bcols, brow, woutT, g1, g2 = _prep_weights(
        L0_attn_w_qkv, L0_attn_b_qkv, L0_attn_w_out,
        L0_ffn_G1_R, L0_ffn_G2_R, L0_ffn_G1_I, L0_ffn_G2_I)
    consts = _host_consts(T)
    shared = {
        "wqkvT": wqkvT.astype(np.float32),
        "bcols": bcols.astype(np.float32),
        "brow": brow.astype(np.float32),
        "woutT": woutT.astype(np.float32),
        "bout": np.asarray(L0_attn_b_out, np.float32).reshape(1, D),
        "g1r": g1["r"].astype(np.float32), "g1i": g1["i"].astype(np.float32),
        "g2r": g2["r"].astype(np.float32), "g2i": g2["i"].astype(np.float32),
        "fbr": np.asarray(L0_ffn_bias_real, np.float32).reshape(1, 512),
        "fbi": np.asarray(L0_ffn_bias_imag, np.float32).reshape(1, 512),
    }
    for k, v in consts.items():
        shared[k] = v
    nc = build_nc(NB, T)
    in_maps = []
    for c in range(NCORES):
        m = dict(shared)
        m["x"] = np.ascontiguousarray(x[NB * c:NB * (c + 1)])
        in_maps.append(m)
    res = run_bass_kernel_spmd(nc, in_maps, core_ids=list(range(NCORES)))
    return np.concatenate([np.asarray(r["y"]) for r in res.results], axis=0)


# version 12
# speedup vs baseline: 2.1203x; 2.1203x over previous
import numpy as np
import contextlib
import ml_dtypes
import concourse.bass as bass
import concourse.mybir as mybir
from concourse.bass_utils import run_bass_kernel_spmd

F32 = mybir.dt.float32
BF16 = mybir.dt.bfloat16
AF = mybir.ActivationFunctionType
ALU = mybir.AluOpType
AX = mybir.AxisListType

NCORES = 8
D = 1024
H = 16
DH = 64
NFREQ = 32
SPW = 512
EPS = 1e-5
NEGBIG = -30000.0

ENGINES = ("sync", "tensor", "vector", "scalar", "gpsimd")



def _qk_perm():
    perm = np.zeros(3 * D, dtype=np.int64)
    r = 0
    for base in (0, D):
        for jb in range(4):
            for eo in (0, 1):
                for hm in range(4):
                    for i in range(NFREQ):
                        perm[r] = base + 64 * (4 * jb + hm) + 2 * i + eo
                        r += 1
    perm[2 * D:] = np.arange(2 * D, 3 * D)
    return perm


def _theta_tables(T):
    inv_freq = 1.0 / 10000.0 ** (np.arange(0, DH, 2, dtype=np.float64) / DH)
    pos = np.arange(T, dtype=np.float64)
    th = pos[None, :] * inv_freq[:, None]
    out = []
    for shift in (0.0, np.pi / 2):
        red = np.mod(th + shift + np.pi, 2 * np.pi) - np.pi
        hi = np.clip(red.astype(np.float32), -3.1415925, 3.1415925)
        lo = (red - hi.astype(np.float64)).astype(np.float32)
        out += [np.tile(hi, (4, 1)).astype(np.float32),
                np.tile(lo, (4, 1)).astype(np.float32)]
    return out


def _host_consts(T):
    c = {}
    c["ident"] = np.eye(128, dtype=np.float32).astype(ml_dtypes.bfloat16)
    c["ones_bf"] = np.ones((128, 128), np.float32).astype(ml_dtypes.bfloat16)
    c["negmask"] = np.triu(np.ones((128, 128), np.float32), k=1) * NEGBIG
    xdg = np.zeros((32, 512), np.float32)
    for d in range(32):
        xdg[d, 128 * (d // 8) + 16 * (d % 8):128 * (d // 8) + 16 * (d % 8) + 16] = 1.0
    c["xdg"] = xdg
    xg = np.zeros((16, 128), np.float32)
    for dl in range(8):
        for g in range(16):
            xg[g, 16 * dl + g] = 1.0
    c["xg"] = xg
    th = _theta_tables(T)
    c["th_sin_hi"], c["th_sin_lo"], c["th_cos_hi"], c["th_cos_lo"] = th
    return c



class Sched:
    def __init__(self, nc, es):
        self.nc = nc
        self.q = {e: [] for e in ENGINES}
        self.cnt = {}
        self.sems = {}
        self.es = es

    def sem(self, name):
        if name not in self.sems:
            self.sems[name] = self.es.enter_context(self.nc.semaphore(name))
            self.cnt[name] = 0
        return self.sems[name]

    def now(self, name):
        self.sem(name)
        return self.cnt[name]

    def add(self, eng, fn, wait=(), inc=None, inc_by=1):
        wait = [(self.sem(s), v) for (s, v) in wait if v > 0]
        itok = None
        if inc is not None:
            self.sem(inc)
            self.cnt[inc] += inc_by
            itok = (self.sems[inc], inc_by)
        self.q[eng].append((fn, wait, itok))
        return self.cnt[inc] if inc else None

    def dma(self, eng, out_ap, in_ap, wait=(), inc=None):
        return self.add(eng, lambda e: e.dma_start(out=out_ap, in_=in_ap),
                        wait=wait, inc=inc, inc_by=16)

    def emit(self, block):
        for eng in ENGINES:
            ops = list(self.q[eng])

            def make(eng_ops):
                def body(e):
                    for fn, wait, itok in eng_ops:
                        for s, v in wait:
                            e.wait_ge(s, v)
                        instr = fn(e)
                        if itok is not None:
                            instr.then_inc(itok[0], itok[1])
                return body
            getattr(block, eng)(make(ops))



def build_nc(NB, T, stop=None, dbg=False):
    TT = T // 128
    NSP = T // SPW
    nc = bass.Bass("TRN2", target_bir_lowering=False,
                   dynamic_dma_scratch_size=16384)
    d_dbg = {}
    if dbg:
        d_dbg["dOn"] = nc.dram_tensor("dOn", [128, 8 * T], BF16,
                                      kind="ExternalOutput").ap()
        d_dbg["dmxl"] = nc.dram_tensor("dmxl", [128, 16], F32,
                                       kind="ExternalOutput").ap()
        d_dbg["dptil"] = nc.dram_tensor("dptil", [128, T], BF16,
                                        kind="ExternalOutput").ap()
        d_dbg["dptT"] = nc.dram_tensor("dptT", [128, T], BF16,
                                       kind="ExternalOutput").ap()
        d_dbg["dmask"] = nc.dram_tensor("dmask", [128, 128], F32,
                                        kind="ExternalOutput").ap()

    d_x = nc.dram_tensor("x", [NB, T, D], F32, kind="ExternalInput").ap()
    d_wqkvT = nc.dram_tensor("wqkvT", [D, 3 * D], F32, kind="ExternalInput").ap()
    d_bcols = nc.dram_tensor("bcols", [128, 16], F32, kind="ExternalInput").ap()
    d_brow = nc.dram_tensor("brow", [1, 3 * D], F32, kind="ExternalInput").ap()
    d_woutT = nc.dram_tensor("woutT", [D, D], F32, kind="ExternalInput").ap()
    d_bout = nc.dram_tensor("bout", [1, D], F32, kind="ExternalInput").ap()
    d_g1 = {k: nc.dram_tensor(f"g1{k}", [4, 32, 32], F32, kind="ExternalInput").ap()
            for k in "ri"}
    d_g2 = {k: nc.dram_tensor(f"g2{k}", [4, 16, 16], F32, kind="ExternalInput").ap()
            for k in "ri"}
    d_fb = {k: nc.dram_tensor(f"fb{k}", [1, 512], F32, kind="ExternalInput").ap()
            for k in "ri"}
    d_c = {k: nc.dram_tensor(k, list(v.shape),
                             BF16 if v.dtype == ml_dtypes.bfloat16 else F32,
                             kind="ExternalInput").ap()
           for k, v in _host_consts(T).items()}
    d_y = nc.dram_tensor("y", [NB, T, D], F32, kind="ExternalOutput").ap()

    es = contextlib.ExitStack()
    S = Sched(nc, es)

    def sb(stack, name, shape, dt=F32, side="left"):
        return stack.enter_context(nc.sbuf_tensor(name, shape, dt, side=side))

    with es:
        ident = sb(es, "identS", [128, 128], BF16)
        ones_bf = sb(es, "onesS", [128, 128], BF16)
        negmask = sb(es, "negmaskS", [128, 128], F32)
        bcols = sb(es, "bcolsS", [128, 16], F32)
        boutrow = sb(es, "boutrS", [1, D], BF16)
        fbrow = sb(es, "fbrowS", [1, 1024], BF16)
        wt_r = sb(es, "wtrS", [128, 2048], BF16)
        wt_i = sb(es, "wtiS", [128, 2048], BF16)
        wt_ni = sb(es, "wtniS", [128, 2048], BF16)
        bvb = sb(es, "bvbS", [128, D], F32)
        bvrow = sb(es, "bvrowS", [1, 1024], BF16)
        lns = [sb(es, f"lnsS{p}", [128, 16], F32) for p in range(2)]
        sq_junk = sb(es, "sqjS", [128, D], BF16)
        rs = [sb(es, f"rsS{i}", [128, SPW], F32) for i in range(2)]
        xt_in = [sb(es, f"xtinS{p}", [128, D], F32) for p in range(2)]
        pb = [es.enter_context(nc.psum_tensor(f"pb{i}", [128, 512], F32))
              for i in range(8)]

        for ap_sb, key in ((ident, "ident"), (ones_bf, "ones_bf"),
                           (negmask, "negmask")):
            S.dma("sync", ap_sb.ap(), d_c[key], inc="s_ld")
        S.dma("sync", bcols.ap(), d_bcols, inc="s_ld")

        setup_es = contextlib.ExitStack()
        brow_f = sb(setup_es, "browF", [1, 3 * D], F32, side="right")
        bout_f = sb(setup_es, "boutF", [1, D], F32, side="right")
        fb_f = sb(setup_es, "fbF", [1, 1024], F32, side="right")
        g1tmp = sb(setup_es, "g1tmpS", [32, 128], F32, side="right")
        g2tmp = sb(setup_es, "g2tmpS", [16, 64], F32, side="right")
        g1rep = sb(setup_es, "g1repS", [32, 2048], F32, side="right")
        vcs = sb(setup_es, "vcsS", [128, 2048], F32, side="right")
        wacc = sb(setup_es, "waccS", [128, 512], F32, side="right")
        wtmp = sb(setup_es, "wtmpS", [128, 512], F32, side="right")
        xdg = sb(setup_es, "xdgS", [32, 512], F32, side="right")
        xg = sb(setup_es, "xgS", [16, 128], F32, side="right")

        S.dma("sync", brow_f.ap(), d_brow, inc="s_ld")
        S.dma("sync", bout_f.ap(), d_bout, inc="s_ld")
        S.dma("sync", fb_f.ap()[:, 0:512], d_fb["r"], inc="s_ld")
        S.dma("sync", fb_f.ap()[:, 512:1024], d_fb["i"], inc="s_ld")
        S.dma("sync", xdg.ap(), d_c["xdg"], inc="s_ld")
        S.dma("sync", xg.ap(), d_c["xg"], inc="s_ld")
        nld = S.now("s_ld")

        S.add("vector", lambda e: e.tensor_copy(boutrow.ap(), bout_f.ap()),
              wait=[("s_ld", nld)], inc="s_cst")
        S.add("vector", lambda e: e.tensor_copy(fbrow.ap(), fb_f.ap()), inc="s_cst")

        for mi, mk in enumerate("ri"):
            wdst = wt_r if mi == 0 else wt_i
            for c in range(4):
                wg = [("s_wbrep", S.now("s_wbrep")),
                      ("s_wbmm", S.now("s_wbmm"))] if c == 0 else ()
                S.dma("sync", g1tmp.ap()[:, 32 * c:32 * c + 32], d_g1[mk][c],
                      wait=wg, inc="s_wbld")
                S.dma("sync", g2tmp.ap()[:, 16 * c:16 * c + 16], d_g2[mk][c],
                      inc="s_wbld")
            nldg = S.now("s_wbld")
            for c in range(4):
                for ei in range(16):
                    wg = [("s_wbld", nldg),
                          ("s_wbmm", S.now("s_wbmm"))] if (c == 0 and ei == 0) else ()
                    S.add("vector",
                          lambda e, c=c, ei=ei: e.tensor_copy(
                              bass.AP(g1rep, 512 * c + ei, [[2048, 32], [16, 32]]),
                              g1tmp.ap()[:, 32 * c:32 * c + 32]),
                          wait=wg, inc="s_wbrep")
            nrep = S.now("s_wbrep")
            for c in range(4):
                S.add("tensor",
                      lambda e, c=c: e.matmul(
                          pb[7][0:128, 0:16], xg.ap()[:],
                          g2tmp.ap()[:, 16 * c:16 * c + 16],
                          start=True, stop=True),
                      wait=[("s_ld", nld), ("s_wbld", nldg),
                            ("s_wbv", S.now("s_wbv"))],
                      inc="s_wbmm")
                nmmv = S.now("s_wbmm")
                for bi in range(32):
                    S.add("vector",
                          lambda e, c=c, bi=bi: e.tensor_copy(
                              bass.AP(vcs, 512 * c + 16 * bi, [[2048, 128], [1, 16]]),
                              pb[7][0:128, 0:16]),
                          wait=[("s_wbmm", nmmv),
                                ("s_wbv", S.now("s_wbv"))] if bi == 0 else (),
                          inc="s_wbv")
            for ic in range(4):
                for c in range(4):
                    S.add("tensor",
                          lambda e, ic=ic, c=c: e.matmul(
                              pb[6][0:128, 0:512],
                              xdg.ap()[:, 128 * ic:128 * ic + 128],
                              g1rep.ap()[:, 512 * c:512 * c + 512],
                              start=True, stop=True),
                          wait=[("s_wbrep", nrep), ("s_wbv", S.now("s_wbv"))],
                          inc="s_wbmm")
                    nmmu = S.now("s_wbmm")
                    if c == 0:
                        S.add("vector",
                              lambda e, c=c: e.tensor_mul(
                                  wacc.ap(), pb[6][0:128, 0:512],
                                  vcs.ap()[:, 512 * c:512 * c + 512]),
                              wait=[("s_wbmm", nmmu),
                                    ("s_wbv", S.now("s_wbv"))], inc="s_wbv")
                    else:
                        S.add("vector",
                              lambda e, c=c: e.tensor_mul(
                                  wtmp.ap(), pb[6][0:128, 0:512],
                                  vcs.ap()[:, 512 * c:512 * c + 512]),
                              wait=[("s_wbmm", nmmu),
                                    ("s_wbv", S.now("s_wbv"))], inc="s_wbv")
                        S.add("vector",
                              lambda e: e.tensor_add(wacc.ap(), wacc.ap(),
                                                     wtmp.ap()),
                              wait=[("s_wbv", S.now("s_wbv"))], inc="s_wbv")
                S.add("vector",
                      lambda e, ic=ic, wdst=wdst: e.tensor_copy(
                          wdst.ap()[:, 512 * ic:512 * ic + 512], wacc.ap()),
                      wait=[("s_wbv", S.now("s_wbv"))], inc="s_wbv")
        nwb_mm = S.now("s_wbmm")
        nwb_v = S.now("s_wbv")
        S.add("vector", lambda e: e.tensor_scalar_mul(wt_ni.ap(), wt_i.ap(), -1.0),
              wait=[("s_wbv", nwb_v)], inc="s_cst")

        S.add("vector", lambda e: e.tensor_copy(bvrow.ap(),
                                                brow_f.ap()[0:1, 2 * D:3 * D]),
              wait=[("s_ld", nld)], inc="s_cst")
        ncst_bv = S.now("s_cst")
        for j in range(2):
            S.add("tensor",
                  lambda e, j=j: e.matmul(
                      pb[7][0:128, 0:512], ones_bf.ap()[0:1, 0:128],
                      bvrow.ap()[0:1, 512 * j:512 * j + 512],
                      start=True, stop=True),
                  wait=[("s_cst", ncst_bv), ("s_bvbe", S.now("s_bvbe")),
                        ("s_wbv", nwb_v)],
                  inc="s_bvb")
            S.add("vector",
                  lambda e, j=j: e.tensor_copy(bvb.ap()[:, 512 * j:512 * j + 512],
                                               pb[7][0:128, 0:512]),
                  wait=[("s_bvb", S.now("s_bvb"))], inc="s_bvbe")
        setup_done = dict(wbv=nwb_v, wbmm=nwb_mm, bvbe=S.now("s_bvbe"),
                          cst=S.now("s_cst"), trig=S.now("s_trig"))
        setup_es.close()

        def emit_batch(b, out_total):
            bat_es = contextlib.ExitStack()
            QE = sb(bat_es, f"QE{b}", [128, 4 * T], BF16)
            QO = sb(bat_es, f"QO{b}", [128, 4 * T], BF16)
            KE = sb(bat_es, f"KE{b}", [128, 4 * T], BF16)
            KO = sb(bat_es, f"KO{b}", [128, 4 * T], BF16)
            Vsb = sb(bat_es, f"V{b}", [128, TT * 1024], BF16)

            r1_es = contextlib.ExitStack()
            c4 = sb(r1_es, f"c4_{b}", [128, T], BF16, side="right")
            s4 = sb(r1_es, f"s4_{b}", [128, T], BF16, side="right")
            r2_es = contextlib.ExitStack()
            tstage = {k: sb(r2_es, f"t{b}_{k}", [128, T], F32, side="right")
                      for k in ("th_sin_hi", "th_sin_lo", "th_cos_hi",
                                "th_cos_lo")}
            wtheta = [("s_x3", S.now("s_x3")), ("s_onmv0", S.now("s_onmv0")),
                      ("s_onmv1", S.now("s_onmv1")), ("s_ffn", S.now("s_ffn")),
                      ("s_dbg", S.now("s_dbg"))]
            if b == 0:
                wtheta = [("s_ld", nld), ("s_cst", S.now("s_cst")),
                          ("s_wbmm", nwb_mm), ("s_wbv", nwb_v),
                          ("s_bvb", S.now("s_bvb"))]
            for k in tstage:
                S.dma("sync", tstage[k].ap(), d_c[k], wait=wtheta, inc="s_thld")
            nthld = S.now("s_thld")
            S.add("vector", lambda e, t=tstage: e.tensor_add(
                t["th_sin_hi"].ap(), t["th_sin_hi"].ap(), t["th_sin_lo"].ap()),
                wait=[("s_thld", nthld)], inc="s_th")
            S.add("vector", lambda e, t=tstage: e.tensor_add(
                t["th_cos_hi"].ap(), t["th_cos_hi"].ap(), t["th_cos_lo"].ap()),
                inc="s_th")
            S.add("scalar", lambda e, t=tstage, s4=s4: e.activation(
                out=s4.ap(), in_=t["th_sin_hi"].ap(), func=AF.Sin),
                wait=[("s_th", S.now("s_th"))], inc="s_trig")
            S.add("scalar", lambda e, t=tstage, c4=c4: e.activation(
                out=c4.ap(), in_=t["th_cos_hi"].ap(), func=AF.Sin), inc="s_trig")
            n_trig = S.now("s_trig")
            r2_es.close()
            if stop == "trig":
                r1_es.close()
                bat_es.close()
                return out_total

            p1_es = contextlib.ExitStack()
            xhatT = sb(p1_es, f"xhatT{b}", [128, 8 * T], BF16, side="right")
            xh1 = [sb(p1_es, f"xh1_{b}_0", [128, D], BF16, side="right")] * 2
            wbuf_f = sb(p1_es, f"wbf{b}", [128, 1024], F32, side="right")
            wbuf_b = [sb(p1_es, f"wbb{b}_{p}", [128, 2048], BF16, side="right")
                      for p in range(2)]

            xhat_base = S.now("s_xhat")
            xT_base = S.now("s_xT")

            _tt_lim = TT
            if stop and stop.startswith("p1a:"):
                _tt_lim = int(stop.split(":")[1])
            for tt in range(_tt_lim):
                par = tt % 2
                ln = lns[par]
                w = [("s_xhat", xhat_base + tt - 1),
                     ("s_out0", S.now("s_out0")), ("s_out1", S.now("s_out1"))]
                if tt < 2:
                    w.append(("s_x2", S.now("s_x2")))
                S.dma("sync", xt_in[par].ap(),
                      d_x[b, 128 * tt:128 * tt + 128, :], wait=w,
                      inc=f"s_x{par}")
                nx = S.now(f"s_x{par}")
                sx = f"s_x{par}"
                S.add("scalar",
                      lambda e, par=par, ln=ln: e.activation(
                          out=sq_junk.ap(), in_=xt_in[par].ap(), func=AF.Square,
                          accum_out=ln.ap()[:, 1:2]),
                      wait=[(sx, nx), ("s_trig", n_trig),
                            ("s_lnv", S.now("s_lnv") - 1)],
                      inc="s_stats")
                S.add("vector",
                      lambda e, par=par, ln=ln: e.reduce_sum(
                          ln.ap()[:, 0:1], xt_in[par].ap(), AX.X),
                      wait=[(sx, nx), ("s_lnr", S.now("s_lnr") - 1)],
                      inc="s_stats")

                emit_ln_smalls(S, ln)
                S.add("scalar",
                      lambda e, ln=ln: e.activation(
                          out=ln.ap()[:, 6:7], in_=ln.ap()[:, 5:6], func=AF.Ln),
                      wait=[("s_lnv", S.now("s_lnv"))], inc="s_chs")
                S.add("scalar",
                      lambda e, ln=ln: e.activation(
                          out=ln.ap()[:, 7:8], in_=ln.ap()[:, 6:7], func=AF.Exp,
                          scale=-0.5),
                      wait=[("s_chs", S.now("s_chs"))], inc="s_lnr")
                S.add("vector",
                      lambda e, ln=ln: e.tensor_mul(
                          ln.ap()[:, 9:10], ln.ap()[:, 8:9], ln.ap()[:, 7:8]),
                      wait=[("s_lnr", S.now("s_lnr")),
                            ("s_xhat", S.now("s_xhat") - 1)], inc="s_lnm")
                S.add("scalar",
                      lambda e, par=par, ln=ln: e.activation(
                          out=xh1[par].ap(), in_=xt_in[par].ap(),
                          func=AF.Identity,
                          bias=ln.ap()[:, 9:10], scale=ln.ap()[:, 7:8]),
                      wait=[("s_lnm", S.now("s_lnm")),
                            ("s_xTmm", S.now("s_xTmm"))],
                      inc="s_xhat")
                nxh = S.now("s_xhat")
                for half in range(2):
                    bank = pb[6 + half].ap().bitcast(BF16)
                    for j in range(4):
                        dc = 4 * half + j
                        w = []
                        if j == 0:
                            w = [("s_xhat", nxh),
                                 ("s_xT", xT_base + 2 * tt + half - 1)]
                            if tt == 0:
                                w += [("s_wbv", setup_done["wbv"]),
                                      ("s_bvbe", setup_done["bvbe"]),
                                      ("s_ptev", S.now("s_ptev"))]
                        S.add("tensor",
                              lambda e, par=par, dc=dc, bank=bank, j=j:
                              e.transpose(
                                  bank[0:128, 128 * j:128 * j + 128],
                                  xh1[par].ap()[:, 128 * dc:128 * dc + 128],
                                  ident.ap()),
                              wait=w, inc="s_xTmm")
                    S.add("vector",
                          lambda e, tt=tt, half=half, bank=bank, xhatT=xhatT:
                          e.tensor_copy(
                              bass.AP(xhatT, 4 * T * half + 128 * tt,
                                      [[8 * T, 128], [T, 4], [1, 128]]),
                              bank[0:128, 0:512]),
                          wait=[("s_xTmm", S.now("s_xTmm"))], inc="s_xT")
            nxt_done = S.now("s_xT")
            if stop and stop.startswith("p1a"):
                p1_es.close()
                r1_es.close()
                bat_es.close()
                return out_total

            rope_base = S.now("s_rope")
            vev_base = S.now("s_vev")
            qkdst = {0: ("QE", "QO"), 1: ("KE", "KO")}
            blocks = {"QE": QE, "QO": QO, "KE": KE, "KO": KO}
            for wch in range(12):
                wpar = wch % 2
                for wh in range(2):
                    wload = [("s_wcast", S.now("s_wcast"))]
                    if b == 0 and wch == 0 and wh == 0:
                        wload += [("s_wbmm", nwb_mm), ("s_wbv", nwb_v)]
                    S.dma("sync",
                          wbuf_f.ap().rearrange("p (c f) -> p c f", f=128),
                          d_wqkvT.rearrange("(c p) f -> p c f", p=128)[
                              :, :, 256 * wch + 128 * wh:
                              256 * wch + 128 * wh + 128],
                          wait=wload, inc="s_wld")
                    gate = []
                    if wch >= 2:
                        gate = [("s_qk", S.now("s_qk")),
                                ("s_vmm", S.now("s_vmm"))]
                    S.add("vector",
                          lambda e, wpar=wpar, wh=wh: e.tensor_copy(
                              bass.AP(wbuf_b[wpar], 128 * wh,
                                      [[2048, 128], [256, 8], [1, 128]]),
                              wbuf_f.ap()),
                          wait=[("s_wld", S.now("s_wld"))] + gate,
                          inc="s_wcast")
                ncast = S.now("s_wcast")
                if wch < 8:
                    pair = wch
                    dE, dO = qkdst[pair // 4]
                    jb = pair % 4
                    bcE = bcols.ap()[:, 2 * pair:2 * pair + 1]
                    bcO = bcols.ap()[:, 2 * pair + 1:2 * pair + 2]
                    for sp in range(NSP):
                        ppar = sp % 2
                        for eo in range(2):
                            for dck in range(8):
                                w = []
                                if eo == 0 and dck == 0:
                                    w = [("s_wcast", ncast), ("s_xT", nxt_done),
                                         ("s_rope", S.now("s_rope") - 1),
                                         ("s_x2", S.now("s_x2")),
                                         ("s_fT", S.now("s_fT"))]
                                S.add("tensor",
                                      lambda e, ppar=ppar, eo=eo, dck=dck, sp=sp,
                                      wpar=wpar: e.matmul(
                                          pb[2 * ppar + eo][0:128, 0:SPW],
                                          wbuf_b[wpar].ap()[
                                              :, 256 * dck + 128 * eo:
                                              256 * dck + 128 * eo + 128],
                                          xhatT.ap()[:, T * dck + SPW * sp:
                                                     T * dck + SPW * sp + SPW],
                                          start=(dck == 0), stop=(dck == 7)),
                                      wait=w,
                                      inc="s_qk" if (eo == 1 and dck == 7) else None)
                        nqk = S.now("s_qk")
                        eout = blocks[dE].ap()[:, T * jb + SPW * sp:
                                               T * jb + SPW * sp + SPW]
                        oout = blocks[dO].ap()[:, T * jb + SPW * sp:
                                               T * jb + SPW * sp + SPW]
                        csl = c4.ap()[:, SPW * sp:SPW * sp + SPW]
                        ssl = s4.ap()[:, SPW * sp:SPW * sp + SPW]

                        wr = [("s_qk", nqk), ("s_rope", S.now("s_rope"))]
                        if b == 0 and pair == 0 and sp == 0:
                            wr.append(("s_ld", nld))
                        pE = pb[2 * ppar][0:128, 0:SPW]
                        pO = pb[2 * ppar + 1][0:128, 0:SPW]
                        def rope_a(e, pE=pE, pO=pO, csl=csl, ssl=ssl, bcE=bcE,
                                   bcO=bcO):
                            e.scalar_tensor_tensor(rs[0].ap(), pE, bcE, csl,
                                                   ALU.add, ALU.mult)
                            return e.scalar_tensor_tensor(rs[1].ap(), pO, bcO,
                                                          ssl, ALU.add, ALU.mult)
                        S.add("vector", rope_a, wait=wr, inc="s_chv")
                        S.add("vector",
                              lambda e, eout=eout: e.tensor_sub(
                                  eout, rs[0].ap(), rs[1].ap()),
                              wait=[("s_chv", S.now("s_chv"))], inc="s_chv")
                        def rope_b(e, pE=pE, pO=pO, csl=csl, ssl=ssl, bcE=bcE,
                                   bcO=bcO):
                            e.scalar_tensor_tensor(rs[0].ap(), pE, bcE, ssl,
                                                   ALU.add, ALU.mult)
                            return e.scalar_tensor_tensor(rs[1].ap(), pO, bcO,
                                                          csl, ALU.add, ALU.mult)
                        S.add("vector", rope_b,
                              wait=[("s_chv", S.now("s_chv"))], inc="s_chv")
                        S.add("vector",
                              lambda e, oout=oout: e.tensor_add(
                                  oout, rs[0].ap(), rs[1].ap()),
                              wait=[("s_chv", S.now("s_chv"))], inc="s_rope")
                else:
                    vsp = wch - 8
                    col = 256 * (vsp % 2)
                    for tt in range(TT):
                        vpar = tt % 2
                        for dck in range(8):
                            w = []
                            if dck == 0:
                                w = [("s_wcast", ncast), ("s_xT", nxt_done),
                                     ("s_vev", S.now("s_vev") - 1),
                                     ("s_x3", S.now("s_x3"))]
                            S.add("tensor",
                                  lambda e, vpar=vpar, dck=dck, tt=tt, col=col,
                                  wpar=wpar: e.matmul(
                                      pb[4 + vpar][0:128, col:col + 256],
                                      xhatT.ap()[:, T * dck + 128 * tt:
                                                 T * dck + 128 * tt + 128],
                                      wbuf_b[wpar].ap()[:, 256 * dck:256 * dck + 256],
                                      start=(dck == 0), stop=(dck == 7)),
                                  wait=w, inc="s_vmm" if dck == 7 else None)
                        S.add("vector",
                              lambda e, vpar=vpar, tt=tt, vsp=vsp, col=col,
                              Vsb=Vsb: e.tensor_add(
                                  Vsb.ap().rearrange("p (c h w) -> p c h w",
                                                     c=TT, h=16)[
                                      :, tt:tt + 1, 4 * vsp:4 * vsp + 4, 0:64],
                                  pb[4 + vpar][0:128, col:col + 256].rearrange(
                                      "p (o h w) -> p o h w", o=1, h=4),
                                  bvb.ap()[:, 256 * vsp:256 * vsp + 256].rearrange(
                                      "p (o h w) -> p o h w", o=1, h=4)),
                              wait=[("s_vmm", S.now("s_vmm"))], inc="s_vev")
            n_rope = S.now("s_rope")
            n_vmm = S.now("s_vmm")
            n_vev = S.now("s_vev")
            p1_es.close()
            r1_es.close()
            if stop == "p1b":
                bat_es.close()
                return out_total

            p23_es = contextlib.ExitStack()
            On = sb(p23_es, f"On{b}", [128, 8 * T], BF16, side="right")
            woutT_b = sb(p23_es, f"woT{b}", [128, 8 * D], BF16, side="right")
            wo_f = sb(p23_es, f"woF{b}", [128, 2048], F32, side="right")
            ptil = [sb(p23_es, f"pt{b}_{p}", [128, T], BF16, side="right")
                    for p in range(2)]
            ptT = [sb(p23_es, f"ptT{b}_{p}", [128, T], BF16, side="right")
                   for p in range(2)]
            mxl = [sb(p23_es, f"mxl{b}_{p}", [128, 32], F32, side="right")
                   for p in range(2)]
            ostg = [sb(p23_es, f"ost{b}_{i}", [64, 128], BF16, side="right")
                    for i in range(2)]
            scd = [sb(p23_es, f"scd{b}_{p}", [128, 128], F32, side="right")
                   for p in range(2)]

            for j in range(4):
                S.dma("sync", wo_f.ap().rearrange("p (c f) -> p c f", f=1024),
                      d_woutT.rearrange("(c p) f -> p c f", p=128)[:, 2 * j:2 * j + 2, :],
                      wait=[("s_vmm", n_vmm), ("s_wocast", S.now("s_wocast"))],
                      inc="s_wold")
                S.add("vector",
                      lambda e, j=j: e.tensor_copy(
                          woutT_b.ap()[:, 2048 * j:2048 * j + 2048], wo_f.ap()),
                      wait=[("s_wold", S.now("s_wold"))], inc="s_wocast")
            n_wocast = S.now("s_wocast")

            units = [(tt, hh) for tt in range(TT) for hh in range(16)]
            NU = len(units)
            rec = [dict() for _ in range(NU)]

            def spans_of(tt):
                nk = 128 * (tt + 1)
                ns = (nk + SPW - 1) // SPW
                return [min(SPW, nk - SPW * s) for s in range(ns)]

            def stage_a(i):
                tt, hh = units[i]
                jb, hm = hh // 4, hh % 4
                par = i % 2
                widths = spans_of(tt)
                ns = len(widths)
                b0 = p2st["sc_ctr"]
                p2st["sc_ctr"] += ns
                r = rec[i]
                r["par"], r["ns"], r["widths"], r["b0"] = par, ns, widths, b0
                sc0 = S.now("s_sc")
                for s_ in range(ns):
                    w_ = widths[s_]
                    bk = (b0 + s_) % 4
                    wt = list(p2st["scb"][bk])
                    if i == 0 and s_ == 0:
                        wt += [("s_rope", n_rope)]
                    for eo, (Qb, Kb) in enumerate(((QE, KE), (QO, KO))):
                        S.add("tensor",
                              lambda e, bk=bk, w_=w_, s_=s_, hm=hm, jb=jb,
                              tt=tt, Qb=Qb, Kb=Kb, eo=eo: e.matmul(
                                  pb[bk][0:128, 0:w_],
                                  Qb.ap()[32 * hm:32 * hm + 32,
                                          T * jb + 128 * tt:
                                          T * jb + 128 * tt + 128],
                                  Kb.ap()[32 * hm:32 * hm + 32,
                                          T * jb + SPW * s_:
                                          T * jb + SPW * s_ + w_],
                                  start=(eo == 0), stop=(eo == 1),
                                  tile_position=(32 * hm, 0)),
                              wait=wt if eo == 0 else (),
                              inc="s_sc" if eo == 1 else None)
                r["sc"] = S.now("s_sc")
                ml = mxl[par]
                sc_ = scd[par]
                lb = (b0 + ns - 1) % 4
                doff = widths[-1] - 128
                wg = []
                if i >= 2:
                    wg.append(("s_fct", rec[i - 2]["fct"]))
                if b == 0 and i == 0:
                    wg.append(("s_ld", nld))
                mx0 = S.now("s_mx")
                mi = 0
                for s_ in range(ns - 1):
                    S.add("vector",
                          lambda e, ml=ml, mi=mi, bk=(b0 + s_) % 4,
                          w_=widths[s_]: e.reduce_max(
                              ml.ap()[:, mi:mi + 1], pb[bk][0:128, 0:w_],
                              AX.X),
                          wait=wg + [("s_sc", sc0 + s_ + 1)] if (wg or True)
                          else (), inc="s_chv")
                    wg = []
                    S.add("vector",
                          lambda e, ml=ml, mi=mi: e.tensor_scalar(
                              ml.ap()[:, 6 + mi:7 + mi], ml.ap()[:, mi:mi + 1],
                              -0.125, None, ALU.mult),
                          wait=[("s_chv", S.now("s_chv"))], inc="s_mx")
                    mi += 1
                cpc = S.add("vector",
                            lambda e, lb=lb, doff=doff, sc_=sc_:
                            e.scalar_tensor_tensor(
                                sc_.ap(), pb[lb][0:128, doff:doff + 128],
                                1.0, negmask.ap(), ALU.mult, ALU.add),
                            wait=wg + [("s_sc", sc0 + ns)], inc="s_chv")
                r["cpc"] = cpc
                if doff > 0:
                    S.add("vector",
                          lambda e, ml=ml, mi=mi, lb=lb, doff=doff:
                          e.reduce_max(ml.ap()[:, mi:mi + 1],
                                       pb[lb][0:128, 0:doff], AX.X),
                          inc="s_chv")
                    S.add("vector",
                          lambda e, ml=ml, mi=mi: e.tensor_scalar(
                              ml.ap()[:, 6 + mi:7 + mi], ml.ap()[:, mi:mi + 1],
                              -0.125, None, ALU.mult),
                          wait=[("s_chv", S.now("s_chv"))], inc="s_mx")
                    mi += 1
                S.add("vector",
                      lambda e, ml=ml, mi=mi, sc_=sc_: e.reduce_max(
                          ml.ap()[:, mi:mi + 1], sc_.ap(), AX.X),
                      wait=[("s_chv", cpc)], inc="s_chv")
                S.add("vector",
                      lambda e, ml=ml, mi=mi: e.tensor_scalar(
                          ml.ap()[:, 6 + mi:7 + mi], ml.ap()[:, mi:mi + 1],
                          -0.125, None, ALU.mult),
                      wait=[("s_chv", S.now("s_chv"))], inc="s_mx")
                mi += 1
                nmx = mi
                r["nmx"] = nmx
                S.add("vector",
                      lambda e, ml=ml, nmx=nmx: e.reduce_max(
                          ml.ap()[:, 24:25], ml.ap()[:, 0:nmx], AX.X),
                      wait=[("s_chv", S.now("s_chv"))], inc="s_chv")
                S.add("vector",
                      lambda e, ml=ml: e.tensor_scalar(
                          ml.ap()[:, 25:26], ml.ap()[:, 24:25], -0.125,
                          None, ALU.mult),
                      wait=[("s_chv", S.now("s_chv"))], inc="s_mx")
                r["mxM"] = S.now("s_mx")
                wt0 = []
                if i >= 2:
                    wt0 += [("s_psc", rec[i - 2]["psc"]),
                            ("s_ptT", rec[i - 2]["ptT"])]
                mi = 0
                for s_ in range(ns - 1):
                    bk = (b0 + s_) % 4
                    cnt = S.add("scalar",
                                lambda e, ml=ml, s_=s_, bk=bk, w_=widths[s_],
                                par=par, mi=mi: e.activation(
                                    out=ptil[par].ap()[:, SPW * s_:
                                                       SPW * s_ + w_],
                                    in_=pb[bk][0:128, 0:w_], func=AF.Exp,
                                    scale=0.125,
                                    bias=ml.ap()[:, 6 + mi:7 + mi],
                                    accum_out=ml.ap()[:, 12 + mi:13 + mi]),
                                wait=(wt0 if mi == 0 else [])
                                + [("s_mx", mx0 + mi + 1)], inc="s_exp")
                    p2st["scb"][bk] = [("s_exp", cnt)]
                    mi += 1
                dbase = SPW * (ns - 1)
                lbw = [("s_chv", cpc)]
                if doff > 0:
                    cnt = S.add("scalar",
                                lambda e, ml=ml, lb=lb, doff=doff, par=par,
                                mi=mi, dbase=dbase: e.activation(
                                    out=ptil[par].ap()[:, dbase:dbase + doff],
                                    in_=pb[lb][0:128, 0:doff], func=AF.Exp,
                                    scale=0.125,
                                    bias=ml.ap()[:, 6 + mi:7 + mi],
                                    accum_out=ml.ap()[:, 12 + mi:13 + mi]),
                                wait=(wt0 if mi == 0 else [])
                                + [("s_mx", mx0 + mi + 1)], inc="s_exp")
                    lbw.append(("s_exp", cnt))
                    mi += 1
                S.add("scalar",
                      lambda e, ml=ml, sc_=sc_, par=par, mi=mi,
                      dbase=dbase, doff=doff: e.activation(
                          out=ptil[par].ap()[:, dbase + doff:
                                             dbase + doff + 128],
                          in_=sc_.ap(), func=AF.Exp, scale=0.125,
                          bias=ml.ap()[:, 6 + mi:7 + mi],
                          accum_out=ml.ap()[:, 12 + mi:13 + mi]),
                      wait=(wt0 if mi == 0 else [])
                      + [("s_mx", mx0 + mi + 1)], inc="s_exp")
                p2st["scb"][lb] = lbw
                r["exp"] = S.now("s_exp")
                S.add("scalar",
                      lambda e, ml=ml, nmx=nmx: e.activation(
                          out=ml.ap()[:, 18:18 + nmx], in_=ml.ap()[:, 0:nmx],
                          func=AF.Exp, scale=0.125, bias=ml.ap()[:, 25:26]),
                      wait=[("s_mx", r["mxM"])], inc="s_fct")
                r["fct"] = S.now("s_fct")

            def stage_b_dve(i):
                tt, hh = units[i]
                par, ns, widths = rec[i]["par"], rec[i]["ns"], rec[i]["widths"]
                nmx = rec[i]["nmx"]
                doff = widths[-1] - 128
                ml = mxl[par]
                wls = [("s_fct", rec[i]["fct"])]
                if i >= 2:
                    wls.append(("s_psc", rec[i - 2]["psc"]))
                S.add("vector",
                      lambda e, ml=ml, nmx=nmx: e.tensor_mul(
                          ml.ap()[:, 12:12 + nmx], ml.ap()[:, 12:12 + nmx],
                          ml.ap()[:, 18:18 + nmx]),
                      wait=wls, inc="s_chv")
                S.add("vector",
                      lambda e, ml=ml, nmx=nmx: e.reduce_sum(
                          ml.ap()[:, 26:27], ml.ap()[:, 12:12 + nmx], AX.X),
                      wait=[("s_chv", S.now("s_chv"))], inc="s_chv")
                S.add("vector",
                      lambda e, ml=ml: e.reciprocal(ml.ap()[:, 27:28],
                                                    ml.ap()[:, 26:27]),
                      wait=[("s_chv", S.now("s_chv"))], inc="s_chv")
                S.add("vector",
                      lambda e, ml=ml, nmx=nmx: e.tensor_scalar(
                          ml.ap()[:, 18:18 + nmx], ml.ap()[:, 18:18 + nmx],
                          ml.ap()[:, 27:28], None, ALU.mult),
                      wait=[("s_chv", S.now("s_chv"))], inc="s_lrec")
                rec[i]["lrec"] = S.now("s_lrec")
                psc0 = S.now("s_psc")
                parts = []
                for s_ in range(ns - 1):
                    parts.append((SPW * s_, widths[s_]))
                dbase = SPW * (ns - 1)
                if doff > 0:
                    parts.append((dbase, doff))
                parts.append((dbase + doff, 128))
                for mi, (po, pw) in enumerate(parts):
                    wt = [("s_lrec", rec[i]["lrec"])] if mi == 0 else []
                    S.add("vector",
                          lambda e, ml=ml, mi=mi, po=po, pw=pw, par=par:
                          e.tensor_scalar(
                              ptil[par].ap()[:, po:po + pw],
                              ptil[par].ap()[:, po:po + pw],
                              ml.ap()[:, 18 + mi:19 + mi], None, ALU.mult),
                          wait=wt, inc="s_psc")
                rec[i]["psc"] = S.now("s_psc")
                rec[i]["psc_span"] = [psc0 + s_ + 1 for s_ in range(ns - 1)] \
                    + [psc0 + len(parts)]

            def stage_b(i):
                par, ns, widths = rec[i]["par"], rec[i]["ns"], rec[i]["widths"]
                psc_span = rec[i]["psc_span"]
                ev0 = S.now("s_ptev")
                rec[i]["ev0"] = ev0
                for s_ in range(ns):
                    w_ = widths[s_]
                    k_ = p2st["stg_ctr"]
                    stg = k_ % 2
                    p2st["stg_ctr"] += 1
                    bank = pb[6 + stg].ap().bitcast(BF16)
                    for j in range(w_ // 128):
                        wt = []
                        if j == 0:
                            wt = [("s_psc", psc_span[s_]),
                                  ("s_ptev", k_ - 1)]
                            if i == 0:
                                wt.append(("s_xT", nxt_done))
                        S.add("tensor",
                              lambda e, bank=bank, j=j, s_=s_, par=par:
                              e.transpose(
                                  bank[0:128, 128 * j:128 * j + 128],
                                  ptil[par].ap()[:, SPW * s_ + 128 * j:
                                                 SPW * s_ + 128 * j + 128],
                                  ident.ap()),
                              wait=wt,
                              inc="s_ptT" if j == w_ // 128 - 1 else None)
                    wt = [("s_ptT", S.now("s_ptT"))]
                    if s_ == 0 and i >= 2:
                        wt.append(("s_pvm", rec[i - 2]["pvm"]))
                    S.add("scalar",
                          lambda e, bank=bank, s_=s_, w_=w_, par=par:
                          e.activation(
                              out=ptT[par].ap()[:, SPW * s_:SPW * s_ + w_],
                              in_=bank[0:128, 0:w_], func=AF.Copy),
                          wait=wt, inc="s_ptev")
                rec[i]["ptT"] = S.now("s_ptT")

            def stage_c(i):
                tt, hh = units[i]
                par, ns, widths = rec[i]["par"], rec[i]["ns"], rec[i]["widths"]
                ev0 = rec[i]["ev0"]
                pvb = 4 + (i % 2)
                for c in range(tt + 1):
                    wt = []
                    if c % 4 == 0:
                        wt = [("s_ptev", ev0 + c // 4 + 1)]
                        if c == 0:
                            if i >= 2:
                                wt.append(("s_onev", rec[i - 2]["onev"]))
                            else:
                                wt.append(("s_vev", n_vev))
                    S.add("tensor",
                          lambda e, c=c, hh=hh, par=par, pvb=pvb, tt=tt:
                          e.matmul(
                              pb[pvb][0:64, 0:128],
                              Vsb.ap()[:, 1024 * c + 64 * hh:
                                       1024 * c + 64 * hh + 64],
                              ptT[par].ap()[:, 128 * c:128 * c + 128],
                              start=(c == 0), stop=(c == tt)),
                          wait=wt, inc="s_pvm" if c == tt else None)
                rec[i]["pvm"] = S.now("s_pvm")

            def stage_d(i):
                tt, hh = units[i]
                pvb = 4 + (i % 2)
                dck = hh // 2
                if hh % 2 == 0:
                    S.add("vector",
                          lambda e, pvb=pvb, dck=dck, tt=tt: e.tensor_copy(
                              On.ap()[0:64, T * dck + 128 * tt:
                                      T * dck + 128 * tt + 128],
                              pb[pvb][0:64, 0:128]),
                          wait=[("s_pvm", rec[i]["pvm"])], inc="s_onev")
                else:
                    o = p2st["odd_ctr"] % 2
                    p2st["odd_ctr"] += 1
                    S.add("vector",
                          lambda e, pvb=pvb, o=o: e.tensor_copy(
                              ostg[o].ap(), pb[pvb][0:64, 0:128]),
                          wait=[("s_pvm", rec[i]["pvm"]),
                                (f"s_onmv{o}", p2st["oddb"][o])],
                          inc="s_onev")
                    nv = S.now("s_onev")
                    p2st["oddb"][o] = S.dma(
                        "gpsimd",
                        On.ap()[64:128, T * dck + 128 * tt:
                                T * dck + 128 * tt + 128],
                        ostg[o].ap(), wait=[("s_onev", nv)],
                        inc=f"s_onmv{o}")
                rec[i]["onev"] = S.now("s_onev")

            for g in range(NU + 3):
                if 1 <= g < NU + 1:
                    stage_b_dve(g - 1)
                if g < NU:
                    stage_a(g)
                if 2 <= g < NU + 2:
                    stage_c(g - 2)
                if 1 <= g < NU + 1:
                    stage_b(g - 1)
                if 3 <= g:
                    stage_d(g - 3)
            n_onev = S.now("s_onev")
            n_onmv0 = S.now("s_onmv0")
            n_onmv1 = S.now("s_onmv1")
            if dbg and b == 0:
                wdbg = [("s_onev", n_onev), ("s_onmv0", n_onmv0),
                        ("s_onmv1", n_onmv1)]
                S.dma("scalar", d_dbg["dOn"], On.ap(), wait=wdbg, inc="s_dbg")
                S.dma("scalar", d_dbg["dptil"], ptil[1].ap(), wait=wdbg,
                      inc="s_dbg")
                S.dma("scalar", d_dbg["dptT"], ptT[1].ap(), wait=wdbg,
                      inc="s_dbg")
                S.dma("scalar", d_dbg["dmask"], negmask.ap(), wait=wdbg,
                      inc="s_dbg")
            bat_es.close()
            if stop == "p2":
                p23_es.close()
                return out_total

            p3_es = contextlib.ExitStack()
            x2b = [sb(p3_es, f"x2b{b}_{p}", [128, D], F32) for p in range(2)]
            xh2 = [sb(p3_es, f"xh2{b}_{p}", [128, D], BF16) for p in range(2)]
            xh2T = [sb(p3_es, f"xh2T{b}_{p}", [128, D], BF16) for p in range(2)]
            x3b = [sb(p3_es, f"x3b{b}_{p}", [128, D], F32) for p in range(2)]

            for tt in range(TT):
                par = tt % 2
                ln = lns[par]
                w = [("s_x2", S.now("s_x2") - 2), ("s_xhat", S.now("s_xhat")),
                     ("s_pvm", S.now("s_pvm"))]
                S.dma("sync", xt_in[par].ap(),
                      d_x[b, 128 * tt:128 * tt + 128, :], wait=w,
                      inc=f"s_x{par}")
                nx = S.now(f"s_x{par}")
                sx = f"s_x{par}"
                for fsp in range(2):
                    for dck in range(8):
                        w = []
                        if dck == 0:
                            w = [("s_onmv0", n_onmv0), ("s_onmv1", n_onmv1),
                                 ("s_onev", n_onev),
                                 ("s_wocast", n_wocast),
                                 ("s_x2", S.now("s_x2"))]
                        S.add("tensor",
                              lambda e, fsp=fsp, dck=dck, tt=tt, On=On:
                              e.matmul(
                                  pb[fsp][0:128, 0:512],
                                  On.ap()[:, T * dck + 128 * tt:
                                          T * dck + 128 * tt + 128],
                                  woutT_b.ap()[:, D * dck + 512 * fsp:
                                               D * dck + 512 * fsp + 512],
                                  start=(dck == 0), stop=False),
                              wait=w, inc=None)
                    S.add("tensor",
                          lambda e, fsp=fsp: e.matmul(
                              pb[fsp][0:128, 0:512], ones_bf.ap()[0:1, 0:128],
                              boutrow.ap()[0:1, 512 * fsp:512 * fsp + 512],
                              start=False, stop=True),
                          inc="s_op")
                nop = S.now("s_op")
                for fsp in range(2):
                    S.add("vector",
                          lambda e, fsp=fsp, par=par: e.tensor_add(
                              x2b[par].ap()[:, 512 * fsp:512 * fsp + 512],
                              xt_in[par].ap()[:, 512 * fsp:512 * fsp + 512],
                              pb[fsp][0:128, 0:512]),
                          wait=[("s_op", nop), (sx, nx),
                                ("s_xh2", S.now("s_xh2") - 1)], inc="s_x2")
                nx2 = S.now("s_x2")
                S.add("scalar",
                      lambda e, par=par, ln=ln: e.activation(
                          out=sq_junk.ap(), in_=x2b[par].ap(), func=AF.Square,
                          accum_out=ln.ap()[:, 1:2]),
                      wait=[("s_x2", nx2), ("s_lnv", S.now("s_lnv") - 1)],
                      inc="s_stats")
                S.add("vector",
                      lambda e, par=par, ln=ln: e.reduce_sum(
                          ln.ap()[:, 0:1], x2b[par].ap(), AX.X),
                      wait=[("s_x2", nx2), ("s_lnr", S.now("s_lnr") - 1)],
                      inc="s_stats")
                emit_ln_smalls(S, ln)
                S.add("scalar",
                      lambda e, ln=ln: e.activation(
                          out=ln.ap()[:, 6:7], in_=ln.ap()[:, 5:6], func=AF.Ln),
                      wait=[("s_lnv", S.now("s_lnv"))], inc="s_chs")
                S.add("scalar",
                      lambda e, ln=ln: e.activation(
                          out=ln.ap()[:, 7:8], in_=ln.ap()[:, 6:7], func=AF.Exp,
                          scale=-0.5),
                      wait=[("s_chs", S.now("s_chs"))], inc="s_lnr")
                S.add("vector",
                      lambda e, ln=ln: e.tensor_mul(
                          ln.ap()[:, 9:10], ln.ap()[:, 8:9], ln.ap()[:, 7:8]),
                      wait=[("s_lnr", S.now("s_lnr")),
                            ("s_xh2", S.now("s_xh2") - 1)], inc="s_lnm")
                S.add("scalar",
                      lambda e, par=par, ln=ln: e.activation(
                          out=xh2[par].ap(), in_=x2b[par].ap(),
                          func=AF.Identity,
                          bias=ln.ap()[:, 9:10], scale=ln.ap()[:, 7:8]),
                      wait=[("s_lnm", S.now("s_lnm")),
                            ("s_fTmm", S.now("s_fTmm") - 8)],
                      inc="s_xh2")
                nxh2 = S.now("s_xh2")
                for half2 in range(2):
                    bank = pb[2 + half2].ap().bitcast(BF16)
                    for j in range(4):
                        dc = 4 * half2 + j
                        wtr = []
                        if j == 0:
                            wtr = [("s_xh2", nxh2),
                                   ("s_fT", S.now("s_fT") - 1)]
                        S.add("tensor",
                              lambda e, par=par, dc=dc, bank=bank, j=j:
                              e.transpose(bank[0:128, 128 * j:128 * j + 128],
                                          xh2[par].ap()[:, 128 * dc:128 * dc + 128],
                                          ident.ap()),
                              wait=wtr, inc="s_fTmm")
                    S.add("vector",
                          lambda e, par=par, half2=half2, bank=bank:
                          e.tensor_copy(
                              xh2T[par].ap()[:, 512 * half2:512 * half2 + 512],
                              bank[0:128, 0:512]),
                          wait=[("s_fTmm", S.now("s_fTmm")),
                                ("s_ffn", S.now("s_ffn") - 2)], inc="s_fT")
                nfT = S.now("s_fT")
                for half in range(2):
                    for ic in range(8):
                        if half == 0:
                            wmat = wt_r if ic < 4 else wt_ni
                        else:
                            wmat = wt_i if ic < 4 else wt_r
                        wcol = 512 * (ic % 4)
                        w = []
                        if ic == 0:
                            w = [("s_fT", nfT), ("s_cst", setup_done["cst"]),
                                 ("s_x3", S.now("s_x3"))]
                        S.add("tensor",
                              lambda e, half=half, ic=ic, par=par, wmat=wmat,
                              wcol=wcol: e.matmul(
                                  pb[4 + half][0:128, 0:512],
                                  xh2T[par].ap()[:, 128 * ic:128 * ic + 128],
                                  wmat.ap()[:, wcol:wcol + 512],
                                  start=(ic == 0), stop=False),
                              wait=w, inc=None)
                    S.add("tensor",
                          lambda e, half=half: e.matmul(
                              pb[4 + half][0:128, 0:512],
                              ones_bf.ap()[0:1, 0:128],
                              fbrow.ap()[0:1, 512 * half:512 * half + 512],
                              start=False, stop=True),
                          inc="s_ffn")
                nffn = S.now("s_ffn")
                for half in range(2):
                    S.add("vector",
                          lambda e, half=half, par=par: e.tensor_add(
                              x3b[par].ap()[:, 512 * half:512 * half + 512],
                              x2b[par].ap()[:, 512 * half:512 * half + 512],
                              pb[4 + half][0:128, 0:512]),
                          wait=[("s_ffn", nffn),
                                (f"s_out{par}", S.now(f"s_out{par}"))],
                          inc="s_x3")
                S.dma("gpsimd", d_y[b, 128 * tt:128 * tt + 128, :],
                      x3b[par].ap(), wait=[("s_x3", S.now("s_x3"))],
                      inc=f"s_out{par}")
            out_total += TT
            p3_es.close()
            p23_es.close()
            return out_total

        p2st = {"sc_ctr": 0, "stg_ctr": 0, "odd_ctr": 0,
                "scb": [[], [], [], []], "oddb": [0, 0]}

        out_total = 0
        if stop != "setup":
            for b in range(NB):
                out_total = emit_batch(b, out_total)

        f0, f1 = S.now("s_out0"), S.now("s_out1")
        fd = S.now("s_dbg")
        S.q["gpsimd"].append(
            (lambda e: (e.wait_ge(S.sem("s_out0"), f0),
                        e.wait_ge(S.sem("s_dbg"), fd) if fd else None,
                        e.wait_ge(S.sem("s_out1"), f1))[-1], [], None))

        block = es.enter_context(nc.Block())
        S.emit(block)
    return nc


def emit_ln_smalls(S, ln):
    a = ln.ap()
    S.add("vector", lambda e: e.tensor_scalar_mul(a[:, 2:3], a[:, 0:1], 1.0 / D),
          wait=[("s_stats", S.now("s_stats"))], inc="s_chv")
    S.add("vector", lambda e: e.tensor_scalar(a[:, 4:5], a[:, 1:2], 1.0 / D,
                                              None, ALU.mult))
    S.add("vector", lambda e: e.tensor_mul(a[:, 3:4], a[:, 2:3], a[:, 2:3]),
          wait=[("s_chv", S.now("s_chv"))], inc="s_chv")
    S.add("vector", lambda e: e.tensor_scalar_mul(a[:, 8:9], a[:, 2:3], -1.0))
    S.add("vector", lambda e: e.tensor_scalar(a[:, 5:6], a[:, 4:5], a[:, 3:4],
                                              EPS, ALU.subtract, ALU.add),
          wait=[("s_chv", S.now("s_chv"))], inc="s_lnv")



def _prep_weights(w_qkv, b_qkv, w_out, G1_R, G2_R, G1_I, G2_I):
    perm = _qk_perm()
    wp = np.ascontiguousarray(np.asarray(w_qkv)[perm])
    bp = np.ascontiguousarray(np.asarray(b_qkv)[perm])
    wqkvT = np.ascontiguousarray(wp.T)
    bcols = np.ascontiguousarray(bp[:2048].reshape(16, 128).T)
    brow = bp.reshape(1, 3 * D)
    woutT = np.ascontiguousarray(np.asarray(w_out).T)
    g1 = {"r": np.ascontiguousarray(np.asarray(G1_R)[0].transpose(1, 2, 0)),
          "i": np.ascontiguousarray(np.asarray(G1_I)[0].transpose(1, 2, 0))}
    g2 = {"r": np.ascontiguousarray(np.asarray(G2_R)[:, :, 0, :].transpose(0, 2, 1)),
          "i": np.ascontiguousarray(np.asarray(G2_I)[:, :, 0, :].transpose(0, 2, 1))}
    return wqkvT, bcols, brow, woutT, g1, g2


def kernel(L0_attn_x, L0_attn_w_qkv, L0_attn_b_qkv, L0_attn_w_out, L0_attn_b_out,
           L0_ffn_G1_R, L0_ffn_G2_R, L0_ffn_G1_I, L0_ffn_G2_I,
           L0_ffn_bias_real, L0_ffn_bias_imag):
    x = np.ascontiguousarray(np.asarray(L0_attn_x, dtype=np.float32))
    B, T, _ = x.shape
    NB = B // NCORES
    wqkvT, bcols, brow, woutT, g1, g2 = _prep_weights(
        L0_attn_w_qkv, L0_attn_b_qkv, L0_attn_w_out,
        L0_ffn_G1_R, L0_ffn_G2_R, L0_ffn_G1_I, L0_ffn_G2_I)
    consts = _host_consts(T)
    shared = {
        "wqkvT": wqkvT.astype(np.float32),
        "bcols": bcols.astype(np.float32),
        "brow": brow.astype(np.float32),
        "woutT": woutT.astype(np.float32),
        "bout": np.asarray(L0_attn_b_out, np.float32).reshape(1, D),
        "g1r": g1["r"].astype(np.float32), "g1i": g1["i"].astype(np.float32),
        "g2r": g2["r"].astype(np.float32), "g2i": g2["i"].astype(np.float32),
        "fbr": np.asarray(L0_ffn_bias_real, np.float32).reshape(1, 512),
        "fbi": np.asarray(L0_ffn_bias_imag, np.float32).reshape(1, 512),
    }
    for k, v in consts.items():
        shared[k] = v
    nc = build_nc(NB, T)
    in_maps = []
    for c in range(NCORES):
        m = dict(shared)
        m["x"] = np.ascontiguousarray(x[NB * c:NB * (c + 1)])
        in_maps.append(m)
    res = run_bass_kernel_spmd(nc, in_maps, core_ids=list(range(NCORES)))
    return np.concatenate([np.asarray(r["y"]) for r in res.results], axis=0)


# version 13 (final)
# speedup vs baseline: 2.2187x; 1.0464x over previous
import numpy as np
import contextlib
import ml_dtypes
import concourse.bass as bass
import concourse.mybir as mybir
from concourse.bass_utils import run_bass_kernel_spmd

F32 = mybir.dt.float32
BF16 = mybir.dt.bfloat16
AF = mybir.ActivationFunctionType
ALU = mybir.AluOpType
AX = mybir.AxisListType

NCORES = 8
D = 1024
H = 16
DH = 64
NFREQ = 32
SPW = 512
EPS = 1e-5
NEGBIG = -30000.0

ENGINES = ("sync", "tensor", "vector", "scalar", "gpsimd")



def _qk_perm():
    perm = np.zeros(3 * D, dtype=np.int64)
    r = 0
    for base in (0, D):
        for jb in range(4):
            for eo in (0, 1):
                for hm in range(4):
                    for i in range(NFREQ):
                        perm[r] = base + 64 * (4 * jb + hm) + 2 * i + eo
                        r += 1
    perm[2 * D:] = np.arange(2 * D, 3 * D)
    return perm


def _theta_tables(T):
    inv_freq = 1.0 / 10000.0 ** (np.arange(0, DH, 2, dtype=np.float64) / DH)
    pos = np.arange(T, dtype=np.float64)
    th = pos[None, :] * inv_freq[:, None]
    out = []
    for shift in (0.0, np.pi / 2):
        red = np.mod(th + shift + np.pi, 2 * np.pi) - np.pi
        hi = np.clip(red.astype(np.float32), -3.1415925, 3.1415925)
        lo = (red - hi.astype(np.float64)).astype(np.float32)
        out += [np.tile(hi, (4, 1)).astype(np.float32),
                np.tile(lo, (4, 1)).astype(np.float32)]
    return out


def _host_consts(T):
    c = {}
    c["ident"] = np.eye(128, dtype=np.float32).astype(ml_dtypes.bfloat16)
    c["ones_bf"] = np.ones((128, 128), np.float32).astype(ml_dtypes.bfloat16)
    c["negmask"] = np.triu(np.ones((128, 128), np.float32), k=1) * NEGBIG
    xdg = np.zeros((32, 512), np.float32)
    for d in range(32):
        xdg[d, 128 * (d // 8) + 16 * (d % 8):128 * (d // 8) + 16 * (d % 8) + 16] = 1.0
    c["xdg"] = xdg
    xg = np.zeros((16, 128), np.float32)
    for dl in range(8):
        for g in range(16):
            xg[g, 16 * dl + g] = 1.0
    c["xg"] = xg
    th = _theta_tables(T)
    c["th_sin_hi"], c["th_sin_lo"], c["th_cos_hi"], c["th_cos_lo"] = th
    return c



class Sched:
    def __init__(self, nc, es):
        self.nc = nc
        self.q = {e: [] for e in ENGINES}
        self.cnt = {}
        self.sems = {}
        self.es = es

    def sem(self, name):
        if name not in self.sems:
            self.sems[name] = self.es.enter_context(self.nc.semaphore(name))
            self.cnt[name] = 0
        return self.sems[name]

    def now(self, name):
        self.sem(name)
        return self.cnt[name]

    def add(self, eng, fn, wait=(), inc=None, inc_by=1):
        wait = [(self.sem(s), v) for (s, v) in wait if v > 0]
        itok = None
        if inc is not None:
            self.sem(inc)
            self.cnt[inc] += inc_by
            itok = (self.sems[inc], inc_by)
        self.q[eng].append((fn, wait, itok))
        return self.cnt[inc] if inc else None

    def dma(self, eng, out_ap, in_ap, wait=(), inc=None):
        return self.add(eng, lambda e: e.dma_start(out=out_ap, in_=in_ap),
                        wait=wait, inc=inc, inc_by=16)

    def emit(self, block):
        for eng in ENGINES:
            ops = list(self.q[eng])

            def make(eng_ops):
                def body(e):
                    for fn, wait, itok in eng_ops:
                        for s, v in wait:
                            e.wait_ge(s, v)
                        instr = fn(e)
                        if itok is not None:
                            instr.then_inc(itok[0], itok[1])
                return body
            getattr(block, eng)(make(ops))



def build_nc(NB, T, stop=None, dbg=False):
    TT = T // 128
    NSP = T // SPW
    nc = bass.Bass("TRN2", target_bir_lowering=False,
                   dynamic_dma_scratch_size=16384)
    d_dbg = {}
    if dbg:
        d_dbg["dOn"] = nc.dram_tensor("dOn", [128, 8 * T], BF16,
                                      kind="ExternalOutput").ap()
        d_dbg["dmxl"] = nc.dram_tensor("dmxl", [128, 16], F32,
                                       kind="ExternalOutput").ap()
        d_dbg["dptil"] = nc.dram_tensor("dptil", [128, T], BF16,
                                        kind="ExternalOutput").ap()
        d_dbg["dptT"] = nc.dram_tensor("dptT", [128, T], BF16,
                                       kind="ExternalOutput").ap()
        d_dbg["dmask"] = nc.dram_tensor("dmask", [128, 128], F32,
                                        kind="ExternalOutput").ap()

    d_x = nc.dram_tensor("x", [NB, T, D], F32, kind="ExternalInput").ap()
    d_wqkvT = nc.dram_tensor("wqkvT", [D, 3 * D], F32, kind="ExternalInput").ap()
    d_bcols = nc.dram_tensor("bcols", [128, 16], F32, kind="ExternalInput").ap()
    d_brow = nc.dram_tensor("brow", [1, 3 * D], F32, kind="ExternalInput").ap()
    d_woutT = nc.dram_tensor("woutT", [D, D], F32, kind="ExternalInput").ap()
    d_bout = nc.dram_tensor("bout", [1, D], F32, kind="ExternalInput").ap()
    d_g1 = {k: nc.dram_tensor(f"g1{k}", [4, 32, 32], F32, kind="ExternalInput").ap()
            for k in "ri"}
    d_g2 = {k: nc.dram_tensor(f"g2{k}", [4, 16, 16], F32, kind="ExternalInput").ap()
            for k in "ri"}
    d_fb = {k: nc.dram_tensor(f"fb{k}", [1, 512], F32, kind="ExternalInput").ap()
            for k in "ri"}
    d_c = {k: nc.dram_tensor(k, list(v.shape),
                             BF16 if v.dtype == ml_dtypes.bfloat16 else F32,
                             kind="ExternalInput").ap()
           for k, v in _host_consts(T).items()}
    d_y = nc.dram_tensor("y", [NB, T, D], F32, kind="ExternalOutput").ap()

    es = contextlib.ExitStack()
    S = Sched(nc, es)

    def sb(stack, name, shape, dt=F32, side="left"):
        return stack.enter_context(nc.sbuf_tensor(name, shape, dt, side=side))

    with es:
        ident = sb(es, "identS", [128, 128], BF16)
        ones_bf = sb(es, "onesS", [128, 128], BF16)
        negmask = sb(es, "negmaskS", [128, 128], F32)
        bcols = sb(es, "bcolsS", [128, 16], F32)
        boutrow = sb(es, "boutrS", [1, D], BF16)
        fbrow = sb(es, "fbrowS", [1, 1024], BF16)
        wt_r = sb(es, "wtrS", [128, 2048], BF16)
        wt_i = sb(es, "wtiS", [128, 2048], BF16)
        wt_ni = sb(es, "wtniS", [128, 2048], BF16)
        bvb = sb(es, "bvbS", [128, D], F32)
        bvrow = sb(es, "bvrowS", [1, 1024], BF16)
        lns = [sb(es, f"lnsS{p}", [128, 16], F32) for p in range(2)]
        sq_junk = sb(es, "sqjS", [128, D], BF16)
        rs = [sb(es, f"rsS{i}", [128, SPW], F32) for i in range(2)]
        xt_in = [sb(es, f"xtinS{p}", [128, D], F32) for p in range(2)]
        pb = [es.enter_context(nc.psum_tensor(f"pb{i}", [128, 512], F32))
              for i in range(8)]

        for ap_sb, key in ((ident, "ident"), (ones_bf, "ones_bf"),
                           (negmask, "negmask")):
            S.dma("sync", ap_sb.ap(), d_c[key], inc="s_ld")
        S.dma("sync", bcols.ap(), d_bcols, inc="s_ld")

        setup_es = contextlib.ExitStack()
        brow_f = sb(setup_es, "browF", [1, 3 * D], F32, side="right")
        bout_f = sb(setup_es, "boutF", [1, D], F32, side="right")
        fb_f = sb(setup_es, "fbF", [1, 1024], F32, side="right")
        g1tmp = sb(setup_es, "g1tmpS", [32, 128], F32, side="right")
        g2tmp = sb(setup_es, "g2tmpS", [16, 64], F32, side="right")
        g1rep = sb(setup_es, "g1repS", [32, 2048], F32, side="right")
        vcs = sb(setup_es, "vcsS", [128, 2048], F32, side="right")
        wacc = sb(setup_es, "waccS", [128, 512], F32, side="right")
        wtmp = sb(setup_es, "wtmpS", [128, 512], F32, side="right")
        xdg = sb(setup_es, "xdgS", [32, 512], F32, side="right")
        xg = sb(setup_es, "xgS", [16, 128], F32, side="right")

        S.dma("sync", brow_f.ap(), d_brow, inc="s_ld")
        S.dma("sync", bout_f.ap(), d_bout, inc="s_ld")
        S.dma("sync", fb_f.ap()[:, 0:512], d_fb["r"], inc="s_ld")
        S.dma("sync", fb_f.ap()[:, 512:1024], d_fb["i"], inc="s_ld")
        S.dma("sync", xdg.ap(), d_c["xdg"], inc="s_ld")
        S.dma("sync", xg.ap(), d_c["xg"], inc="s_ld")
        nld = S.now("s_ld")
        bcq = S.add("vector",
                    lambda e: e.tensor_scalar(bcols.ap()[:, 0:8],
                                              bcols.ap()[:, 0:8], 0.125,
                                              None, ALU.mult),
                    wait=[("s_ld", nld)], inc="s_cst")

        S.add("vector", lambda e: e.tensor_copy(boutrow.ap(), bout_f.ap()),
              wait=[("s_ld", nld)], inc="s_cst")
        S.add("vector", lambda e: e.tensor_copy(fbrow.ap(), fb_f.ap()), inc="s_cst")

        for mi, mk in enumerate("ri"):
            wdst = wt_r if mi == 0 else wt_i
            for c in range(4):
                wg = [("s_wbrep", S.now("s_wbrep")),
                      ("s_wbmm", S.now("s_wbmm"))] if c == 0 else ()
                S.dma("sync", g1tmp.ap()[:, 32 * c:32 * c + 32], d_g1[mk][c],
                      wait=wg, inc="s_wbld")
                S.dma("sync", g2tmp.ap()[:, 16 * c:16 * c + 16], d_g2[mk][c],
                      inc="s_wbld")
            nldg = S.now("s_wbld")
            for c in range(4):
                for ei in range(16):
                    wg = [("s_wbld", nldg),
                          ("s_wbmm", S.now("s_wbmm"))] if (c == 0 and ei == 0) else ()
                    S.add("vector",
                          lambda e, c=c, ei=ei: e.tensor_copy(
                              bass.AP(g1rep, 512 * c + ei, [[2048, 32], [16, 32]]),
                              g1tmp.ap()[:, 32 * c:32 * c + 32]),
                          wait=wg, inc="s_wbrep")
            nrep = S.now("s_wbrep")
            for c in range(4):
                S.add("tensor",
                      lambda e, c=c: e.matmul(
                          pb[7][0:128, 0:16], xg.ap()[:],
                          g2tmp.ap()[:, 16 * c:16 * c + 16],
                          start=True, stop=True),
                      wait=[("s_ld", nld), ("s_wbld", nldg),
                            ("s_wbv", S.now("s_wbv"))],
                      inc="s_wbmm")
                nmmv = S.now("s_wbmm")
                for bi in range(32):
                    S.add("vector",
                          lambda e, c=c, bi=bi: e.tensor_copy(
                              bass.AP(vcs, 512 * c + 16 * bi, [[2048, 128], [1, 16]]),
                              pb[7][0:128, 0:16]),
                          wait=[("s_wbmm", nmmv),
                                ("s_wbv", S.now("s_wbv"))] if bi == 0 else (),
                          inc="s_wbv")
            for ic in range(4):
                for c in range(4):
                    S.add("tensor",
                          lambda e, ic=ic, c=c: e.matmul(
                              pb[6][0:128, 0:512],
                              xdg.ap()[:, 128 * ic:128 * ic + 128],
                              g1rep.ap()[:, 512 * c:512 * c + 512],
                              start=True, stop=True),
                          wait=[("s_wbrep", nrep), ("s_wbv", S.now("s_wbv"))],
                          inc="s_wbmm")
                    nmmu = S.now("s_wbmm")
                    if c == 0:
                        S.add("vector",
                              lambda e, c=c: e.tensor_mul(
                                  wacc.ap(), pb[6][0:128, 0:512],
                                  vcs.ap()[:, 512 * c:512 * c + 512]),
                              wait=[("s_wbmm", nmmu),
                                    ("s_wbv", S.now("s_wbv"))], inc="s_wbv")
                    else:
                        S.add("vector",
                              lambda e, c=c: e.tensor_mul(
                                  wtmp.ap(), pb[6][0:128, 0:512],
                                  vcs.ap()[:, 512 * c:512 * c + 512]),
                              wait=[("s_wbmm", nmmu),
                                    ("s_wbv", S.now("s_wbv"))], inc="s_wbv")
                        S.add("vector",
                              lambda e: e.tensor_add(wacc.ap(), wacc.ap(),
                                                     wtmp.ap()),
                              wait=[("s_wbv", S.now("s_wbv"))], inc="s_wbv")
                S.add("vector",
                      lambda e, ic=ic, wdst=wdst: e.tensor_copy(
                          wdst.ap()[:, 512 * ic:512 * ic + 512], wacc.ap()),
                      wait=[("s_wbv", S.now("s_wbv"))], inc="s_wbv")
        nwb_mm = S.now("s_wbmm")
        nwb_v = S.now("s_wbv")
        S.add("vector", lambda e: e.tensor_scalar_mul(wt_ni.ap(), wt_i.ap(), -1.0),
              wait=[("s_wbv", nwb_v)], inc="s_cst")

        S.add("vector", lambda e: e.tensor_copy(bvrow.ap(),
                                                brow_f.ap()[0:1, 2 * D:3 * D]),
              wait=[("s_ld", nld)], inc="s_cst")
        ncst_bv = S.now("s_cst")
        for j in range(2):
            S.add("tensor",
                  lambda e, j=j: e.matmul(
                      pb[7][0:128, 0:512], ones_bf.ap()[0:1, 0:128],
                      bvrow.ap()[0:1, 512 * j:512 * j + 512],
                      start=True, stop=True),
                  wait=[("s_cst", ncst_bv), ("s_bvbe", S.now("s_bvbe")),
                        ("s_wbv", nwb_v)],
                  inc="s_bvb")
            S.add("vector",
                  lambda e, j=j: e.tensor_copy(bvb.ap()[:, 512 * j:512 * j + 512],
                                               pb[7][0:128, 0:512]),
                  wait=[("s_bvb", S.now("s_bvb"))], inc="s_bvbe")
        setup_done = dict(wbv=nwb_v, wbmm=nwb_mm, bvbe=S.now("s_bvbe"),
                          cst=S.now("s_cst"), trig=S.now("s_trig"))
        setup_es.close()

        def emit_batch(b, out_total):
            bat_es = contextlib.ExitStack()
            QE = sb(bat_es, f"QE{b}", [128, 4 * T], BF16)
            QO = sb(bat_es, f"QO{b}", [128, 4 * T], BF16)
            KE = sb(bat_es, f"KE{b}", [128, 4 * T], BF16)
            KO = sb(bat_es, f"KO{b}", [128, 4 * T], BF16)
            Vsb = sb(bat_es, f"V{b}", [128, TT * 1024], BF16)

            r1_es = contextlib.ExitStack()
            c4 = sb(r1_es, f"c4_{b}", [128, T], BF16, side="right")
            s4 = sb(r1_es, f"s4_{b}", [128, T], BF16, side="right")
            r2_es = contextlib.ExitStack()
            tstage = {k: sb(r2_es, f"t{b}_{k}", [128, T], F32, side="right")
                      for k in ("th_sin_hi", "th_sin_lo", "th_cos_hi",
                                "th_cos_lo")}
            wtheta = [("s_x3", S.now("s_x3")), ("s_onmv0", S.now("s_onmv0")),
                      ("s_onmv1", S.now("s_onmv1")), ("s_ffn", S.now("s_ffn")),
                      ("s_dbg", S.now("s_dbg"))]
            if b == 0:
                wtheta = [("s_ld", nld), ("s_cst", S.now("s_cst")),
                          ("s_wbmm", nwb_mm), ("s_wbv", nwb_v),
                          ("s_bvb", S.now("s_bvb"))]
            for k in tstage:
                S.dma("sync", tstage[k].ap(), d_c[k], wait=wtheta, inc="s_thld")
            nthld = S.now("s_thld")
            S.add("vector", lambda e, t=tstage: e.tensor_add(
                t["th_sin_hi"].ap(), t["th_sin_hi"].ap(), t["th_sin_lo"].ap()),
                wait=[("s_thld", nthld)], inc="s_th")
            S.add("vector", lambda e, t=tstage: e.tensor_add(
                t["th_cos_hi"].ap(), t["th_cos_hi"].ap(), t["th_cos_lo"].ap()),
                inc="s_th")
            S.add("scalar", lambda e, t=tstage, s4=s4: e.activation(
                out=s4.ap(), in_=t["th_sin_hi"].ap(), func=AF.Sin),
                wait=[("s_th", S.now("s_th"))], inc="s_trig")
            S.add("scalar", lambda e, t=tstage, c4=c4: e.activation(
                out=c4.ap(), in_=t["th_cos_hi"].ap(), func=AF.Sin), inc="s_trig")
            n_trig = S.now("s_trig")
            r2_es.close()
            if stop == "trig":
                r1_es.close()
                bat_es.close()
                return out_total

            p1_es = contextlib.ExitStack()
            xhatT = sb(p1_es, f"xhatT{b}", [128, 8 * T], BF16, side="right")
            xh1 = [sb(p1_es, f"xh1_{b}_0", [128, D], BF16, side="right")] * 2
            wbuf_f = sb(p1_es, f"wbf{b}", [128, 1024], F32, side="right")
            wbuf_b = [sb(p1_es, f"wbb{b}_{p}", [128, 2048], BF16, side="right")
                      for p in range(2)]

            xhat_base = S.now("s_xhat")
            xT_base = S.now("s_xT")

            _tt_lim = TT
            if stop and stop.startswith("p1a:"):
                _tt_lim = int(stop.split(":")[1])
            for tt in range(_tt_lim):
                par = tt % 2
                ln = lns[par]
                w = [("s_xhat", xhat_base + tt - 1),
                     ("s_out0", S.now("s_out0")), ("s_out1", S.now("s_out1"))]
                if tt < 2:
                    w.append(("s_x2", S.now("s_x2")))
                S.dma("sync", xt_in[par].ap(),
                      d_x[b, 128 * tt:128 * tt + 128, :], wait=w,
                      inc=f"s_x{par}")
                nx = S.now(f"s_x{par}")
                sx = f"s_x{par}"
                S.add("scalar",
                      lambda e, par=par, ln=ln: e.activation(
                          out=sq_junk.ap(), in_=xt_in[par].ap(), func=AF.Square,
                          accum_out=ln.ap()[:, 1:2]),
                      wait=[(sx, nx), ("s_trig", n_trig),
                            ("s_lnv", S.now("s_lnv") - 1)],
                      inc="s_stats")
                S.add("vector",
                      lambda e, par=par, ln=ln: e.reduce_sum(
                          ln.ap()[:, 0:1], xt_in[par].ap(), AX.X),
                      wait=[(sx, nx), ("s_lnr", S.now("s_lnr") - 1)],
                      inc="s_stats")

                emit_ln_smalls(S, ln)
                S.add("scalar",
                      lambda e, ln=ln: e.activation(
                          out=ln.ap()[:, 6:7], in_=ln.ap()[:, 5:6], func=AF.Ln),
                      wait=[("s_lnv", S.now("s_lnv"))], inc="s_chs")
                S.add("scalar",
                      lambda e, ln=ln: e.activation(
                          out=ln.ap()[:, 7:8], in_=ln.ap()[:, 6:7], func=AF.Exp,
                          scale=-0.5),
                      wait=[("s_chs", S.now("s_chs"))], inc="s_lnr")
                S.add("vector",
                      lambda e, ln=ln: e.tensor_mul(
                          ln.ap()[:, 9:10], ln.ap()[:, 8:9], ln.ap()[:, 7:8]),
                      wait=[("s_lnr", S.now("s_lnr")),
                            ("s_xhat", S.now("s_xhat") - 1)], inc="s_lnm")
                S.add("scalar",
                      lambda e, par=par, ln=ln: e.activation(
                          out=xh1[par].ap(), in_=xt_in[par].ap(),
                          func=AF.Identity,
                          bias=ln.ap()[:, 9:10], scale=ln.ap()[:, 7:8]),
                      wait=[("s_lnm", S.now("s_lnm")),
                            ("s_xTmm", S.now("s_xTmm"))],
                      inc="s_xhat")
                nxh = S.now("s_xhat")
                for half in range(2):
                    bank = pb[6 + half].ap().bitcast(BF16)
                    for j in range(4):
                        dc = 4 * half + j
                        w = []
                        if j == 0:
                            w = [("s_xhat", nxh),
                                 ("s_xT", xT_base + 2 * tt + half - 1)]
                            if tt == 0:
                                w += [("s_wbv", setup_done["wbv"]),
                                      ("s_bvbe", setup_done["bvbe"]),
                                      ("s_ptev", S.now("s_ptev"))]
                        S.add("tensor",
                              lambda e, par=par, dc=dc, bank=bank, j=j:
                              e.transpose(
                                  bank[0:128, 128 * j:128 * j + 128],
                                  xh1[par].ap()[:, 128 * dc:128 * dc + 128],
                                  ident.ap()),
                              wait=w, inc="s_xTmm")
                    S.add("vector",
                          lambda e, tt=tt, half=half, bank=bank, xhatT=xhatT:
                          e.tensor_copy(
                              bass.AP(xhatT, 4 * T * half + 128 * tt,
                                      [[8 * T, 128], [T, 4], [1, 128]]),
                              bank[0:128, 0:512]),
                          wait=[("s_xTmm", S.now("s_xTmm"))], inc="s_xT")
            nxt_done = S.now("s_xT")
            if stop and stop.startswith("p1a"):
                p1_es.close()
                r1_es.close()
                bat_es.close()
                return out_total

            rope_base = S.now("s_rope")
            vev_base = S.now("s_vev")
            qkdst = {0: ("QE", "QO"), 1: ("KE", "KO")}
            blocks = {"QE": QE, "QO": QO, "KE": KE, "KO": KO}
            for wch in range(12):
                wpar = wch % 2
                for wh in range(2):
                    wload = [("s_wcast", S.now("s_wcast"))]
                    if b == 0 and wch == 0 and wh == 0:
                        wload += [("s_wbmm", nwb_mm), ("s_wbv", nwb_v)]
                    S.dma("sync",
                          wbuf_f.ap().rearrange("p (c f) -> p c f", f=128),
                          d_wqkvT.rearrange("(c p) f -> p c f", p=128)[
                              :, :, 256 * wch + 128 * wh:
                              256 * wch + 128 * wh + 128],
                          wait=wload, inc="s_wld")
                    gate = []
                    if wch >= 2:
                        gate = [("s_qk", S.now("s_qk")),
                                ("s_vmm", S.now("s_vmm"))]
                    if wch < 4:
                        S.add("vector",
                              lambda e, wpar=wpar, wh=wh: e.tensor_scalar(
                                  bass.AP(wbuf_b[wpar], 128 * wh,
                                          [[2048, 128], [256, 8], [1, 128]]),
                                  wbuf_f.ap(), 0.125, None, ALU.mult),
                              wait=[("s_wld", S.now("s_wld"))] + gate,
                              inc="s_wcast")
                    else:
                        S.add("vector",
                              lambda e, wpar=wpar, wh=wh: e.tensor_copy(
                                  bass.AP(wbuf_b[wpar], 128 * wh,
                                          [[2048, 128], [256, 8], [1, 128]]),
                                  wbuf_f.ap()),
                              wait=[("s_wld", S.now("s_wld"))] + gate,
                              inc="s_wcast")
                ncast = S.now("s_wcast")
                if wch < 8:
                    pair = wch
                    dE, dO = qkdst[pair // 4]
                    jb = pair % 4
                    bcE = bcols.ap()[:, 2 * pair:2 * pair + 1]
                    bcO = bcols.ap()[:, 2 * pair + 1:2 * pair + 2]
                    for sp in range(NSP):
                        ppar = sp % 2
                        for eo in range(2):
                            for dck in range(8):
                                w = []
                                if eo == 0 and dck == 0:
                                    w = [("s_wcast", ncast), ("s_xT", nxt_done),
                                         ("s_rope", S.now("s_rope") - 1),
                                         ("s_x2", S.now("s_x2")),
                                         ("s_fT", S.now("s_fT"))]
                                S.add("tensor",
                                      lambda e, ppar=ppar, eo=eo, dck=dck, sp=sp,
                                      wpar=wpar: e.matmul(
                                          pb[2 * ppar + eo][0:128, 0:SPW],
                                          wbuf_b[wpar].ap()[
                                              :, 256 * dck + 128 * eo:
                                              256 * dck + 128 * eo + 128],
                                          xhatT.ap()[:, T * dck + SPW * sp:
                                                     T * dck + SPW * sp + SPW],
                                          start=(dck == 0), stop=(dck == 7)),
                                      wait=w,
                                      inc="s_qk" if (eo == 1 and dck == 7) else None)
                        nqk = S.now("s_qk")
                        eout = blocks[dE].ap()[:, T * jb + SPW * sp:
                                               T * jb + SPW * sp + SPW]
                        oout = blocks[dO].ap()[:, T * jb + SPW * sp:
                                               T * jb + SPW * sp + SPW]
                        csl = c4.ap()[:, SPW * sp:SPW * sp + SPW]
                        ssl = s4.ap()[:, SPW * sp:SPW * sp + SPW]

                        wr = [("s_qk", nqk), ("s_rope", S.now("s_rope"))]
                        if b == 0 and pair == 0 and sp == 0:
                            wr.append(("s_ld", nld))
                            wr.append(("s_cst", bcq))
                        pE = pb[2 * ppar][0:128, 0:SPW]
                        pO = pb[2 * ppar + 1][0:128, 0:SPW]
                        def rope_a(e, pE=pE, pO=pO, csl=csl, ssl=ssl, bcE=bcE,
                                   bcO=bcO):
                            e.scalar_tensor_tensor(rs[0].ap(), pE, bcE, csl,
                                                   ALU.add, ALU.mult)
                            return e.scalar_tensor_tensor(rs[1].ap(), pO, bcO,
                                                          ssl, ALU.add, ALU.mult)
                        S.add("vector", rope_a, wait=wr, inc="s_chv")
                        S.add("vector",
                              lambda e, eout=eout: e.tensor_sub(
                                  eout, rs[0].ap(), rs[1].ap()),
                              wait=[("s_chv", S.now("s_chv"))], inc="s_chv")
                        def rope_b(e, pE=pE, pO=pO, csl=csl, ssl=ssl, bcE=bcE,
                                   bcO=bcO):
                            e.scalar_tensor_tensor(rs[0].ap(), pE, bcE, ssl,
                                                   ALU.add, ALU.mult)
                            return e.scalar_tensor_tensor(rs[1].ap(), pO, bcO,
                                                          csl, ALU.add, ALU.mult)
                        S.add("vector", rope_b,
                              wait=[("s_chv", S.now("s_chv"))], inc="s_chv")
                        S.add("vector",
                              lambda e, oout=oout: e.tensor_add(
                                  oout, rs[0].ap(), rs[1].ap()),
                              wait=[("s_chv", S.now("s_chv"))], inc="s_rope")
                else:
                    vsp = wch - 8
                    col = 256 * (vsp % 2)
                    for tt in range(TT):
                        vpar = tt % 2
                        for dck in range(8):
                            w = []
                            if dck == 0:
                                w = [("s_wcast", ncast), ("s_xT", nxt_done),
                                     ("s_vev", S.now("s_vev") - 1),
                                     ("s_x3", S.now("s_x3"))]
                            S.add("tensor",
                                  lambda e, vpar=vpar, dck=dck, tt=tt, col=col,
                                  wpar=wpar: e.matmul(
                                      pb[4 + vpar][0:128, col:col + 256],
                                      xhatT.ap()[:, T * dck + 128 * tt:
                                                 T * dck + 128 * tt + 128],
                                      wbuf_b[wpar].ap()[:, 256 * dck:256 * dck + 256],
                                      start=(dck == 0), stop=(dck == 7)),
                                  wait=w, inc="s_vmm" if dck == 7 else None)
                        S.add("vector",
                              lambda e, vpar=vpar, tt=tt, vsp=vsp, col=col,
                              Vsb=Vsb: e.tensor_add(
                                  Vsb.ap().rearrange("p (c h w) -> p c h w",
                                                     c=TT, h=16)[
                                      :, tt:tt + 1, 4 * vsp:4 * vsp + 4, 0:64],
                                  pb[4 + vpar][0:128, col:col + 256].rearrange(
                                      "p (o h w) -> p o h w", o=1, h=4),
                                  bvb.ap()[:, 256 * vsp:256 * vsp + 256].rearrange(
                                      "p (o h w) -> p o h w", o=1, h=4)),
                              wait=[("s_vmm", S.now("s_vmm"))], inc="s_vev")
            n_rope = S.now("s_rope")
            n_vmm = S.now("s_vmm")
            n_vev = S.now("s_vev")
            p1_es.close()
            r1_es.close()
            if stop == "p1b":
                bat_es.close()
                return out_total

            p23_es = contextlib.ExitStack()
            On = sb(p23_es, f"On{b}", [128, 8 * T], BF16, side="right")
            woutT_b = sb(p23_es, f"woT{b}", [128, 8 * D], BF16, side="right")
            wo_f = sb(p23_es, f"woF{b}", [128, 2048], F32, side="right")
            ptil = [sb(p23_es, f"pt{b}_{p}", [128, T], BF16, side="right")
                    for p in range(2)]
            ptT = [sb(p23_es, f"ptT{b}_{p}", [128, T], BF16, side="right")
                   for p in range(2)]
            mxl = [sb(p23_es, f"mxl{b}_{p}", [128, 32], F32, side="right")
                   for p in range(2)]
            ostg = [sb(p23_es, f"ost{b}_{i}", [64, 128], BF16, side="right")
                    for i in range(2)]
            scd = [sb(p23_es, f"scd{b}_{p}", [128, 128], F32, side="right")
                   for p in range(2)]

            for j in range(4):
                S.dma("sync", wo_f.ap().rearrange("p (c f) -> p c f", f=1024),
                      d_woutT.rearrange("(c p) f -> p c f", p=128)[:, 2 * j:2 * j + 2, :],
                      wait=[("s_vmm", n_vmm), ("s_wocast", S.now("s_wocast"))],
                      inc="s_wold")
                S.add("vector",
                      lambda e, j=j: e.tensor_copy(
                          woutT_b.ap()[:, 2048 * j:2048 * j + 2048], wo_f.ap()),
                      wait=[("s_wold", S.now("s_wold"))], inc="s_wocast")
            n_wocast = S.now("s_wocast")

            units = [(tt, hh) for tt in range(TT) for hh in range(16)]
            NU = len(units)
            rec = [dict() for _ in range(NU)]

            def spans_of(tt):
                nk = 128 * (tt + 1)
                ns = (nk + SPW - 1) // SPW
                return [min(SPW, nk - SPW * s) for s in range(ns)]

            def stage_a(i):
                tt, hh = units[i]
                jb, hm = hh // 4, hh % 4
                par = i % 2
                widths = spans_of(tt)
                ns = len(widths)
                b0 = p2st["sc_ctr"]
                p2st["sc_ctr"] += ns
                r = rec[i]
                r["par"], r["ns"], r["widths"], r["b0"] = par, ns, widths, b0
                sc0 = S.now("s_sc")
                for s_ in range(ns):
                    w_ = widths[s_]
                    bk = (b0 + s_) % 4
                    wt = list(p2st["scb"][bk])
                    if i == 0 and s_ == 0:
                        wt += [("s_rope", n_rope)]
                    for eo, (Qb, Kb) in enumerate(((QE, KE), (QO, KO))):
                        S.add("tensor",
                              lambda e, bk=bk, w_=w_, s_=s_, hm=hm, jb=jb,
                              tt=tt, Qb=Qb, Kb=Kb, eo=eo: e.matmul(
                                  pb[bk][0:128, 0:w_],
                                  Qb.ap()[32 * hm:32 * hm + 32,
                                          T * jb + 128 * tt:
                                          T * jb + 128 * tt + 128],
                                  Kb.ap()[32 * hm:32 * hm + 32,
                                          T * jb + SPW * s_:
                                          T * jb + SPW * s_ + w_],
                                  start=(eo == 0), stop=(eo == 1),
                                  tile_position=(32 * hm, 0)),
                              wait=wt if eo == 0 else (),
                              inc="s_sc" if eo == 1 else None)
                r["sc"] = S.now("s_sc")
                ml = mxl[par]
                sc_ = scd[par]
                lb = (b0 + ns - 1) % 4
                doff = widths[-1] - 128
                wg = []
                if i >= 2:
                    wg.append(("s_fct", rec[i - 2]["fct"]))
                if b == 0 and i == 0:
                    wg.append(("s_ld", nld))
                mx0 = S.now("s_mx")
                mi = 0
                for s_ in range(ns - 1):
                    S.add("vector",
                          lambda e, ml=ml, mi=mi, bk=(b0 + s_) % 4,
                          w_=widths[s_]: e.reduce_max(
                              ml.ap()[:, mi:mi + 1], pb[bk][0:128, 0:w_],
                              AX.X, negate=True),
                          wait=wg + [("s_sc", sc0 + s_ + 1)], inc="s_mx")
                    wg = []
                    mi += 1
                cpc = S.add("vector",
                            lambda e, lb=lb, doff=doff, sc_=sc_:
                            e.scalar_tensor_tensor(
                                sc_.ap(), pb[lb][0:128, doff:doff + 128],
                                1.0, negmask.ap(), ALU.mult, ALU.add),
                            wait=wg + [("s_sc", sc0 + ns)], inc="s_chv")
                r["cpc"] = cpc
                if doff > 0:
                    S.add("vector",
                          lambda e, ml=ml, mi=mi, lb=lb, doff=doff:
                          e.reduce_max(ml.ap()[:, mi:mi + 1],
                                       pb[lb][0:128, 0:doff], AX.X,
                                       negate=True),
                          inc="s_mx")
                    mi += 1
                S.add("vector",
                      lambda e, ml=ml, mi=mi, sc_=sc_: e.reduce_max(
                          ml.ap()[:, mi:mi + 1], sc_.ap(), AX.X, negate=True),
                      wait=[("s_chv", cpc)], inc="s_mx")
                mi += 1
                nmx = mi
                r["nmx"] = nmx
                S.add("vector",
                      lambda e, ml=ml, nmx=nmx: e.tensor_scalar(
                          ml.ap()[:, 6:6 + nmx], ml.ap()[:, 0:nmx], -1.0,
                          None, ALU.mult),
                      wait=[("s_mx", mx0 + nmx)], inc="s_chv")
                S.add("vector",
                      lambda e, ml=ml, nmx=nmx: e.reduce_max(
                          ml.ap()[:, 24:25], ml.ap()[:, 6:6 + nmx], AX.X),
                      wait=[("s_chv", S.now("s_chv"))], inc="s_chv")
                S.add("vector",
                      lambda e, ml=ml: e.tensor_scalar(
                          ml.ap()[:, 25:26], ml.ap()[:, 24:25], -1.0,
                          None, ALU.mult),
                      wait=[("s_chv", S.now("s_chv"))], inc="s_mx")
                r["mxM"] = S.now("s_mx")
                wt0 = []
                if i >= 2:
                    wt0 += [("s_psc", rec[i - 2]["psc"]),
                            ("s_ptT", rec[i - 2]["ptT"])]
                mi = 0
                for s_ in range(ns - 1):
                    bk = (b0 + s_) % 4
                    cnt = S.add("scalar",
                                lambda e, ml=ml, s_=s_, bk=bk, w_=widths[s_],
                                par=par, mi=mi: e.activation(
                                    out=ptil[par].ap()[:, SPW * s_:
                                                       SPW * s_ + w_],
                                    in_=pb[bk][0:128, 0:w_], func=AF.Exp,
                                    bias=ml.ap()[:, mi:mi + 1],
                                    accum_out=ml.ap()[:, 12 + mi:13 + mi]),
                                wait=(wt0 if mi == 0 else [])
                                + [("s_mx", mx0 + mi + 1)], inc="s_exp")
                    p2st["scb"][bk] = [("s_exp", cnt)]
                    mi += 1
                dbase = SPW * (ns - 1)
                lbw = [("s_chv", cpc)]
                if doff > 0:
                    cnt = S.add("scalar",
                                lambda e, ml=ml, lb=lb, doff=doff, par=par,
                                mi=mi, dbase=dbase: e.activation(
                                    out=ptil[par].ap()[:, dbase:dbase + doff],
                                    in_=pb[lb][0:128, 0:doff], func=AF.Exp,
                                    bias=ml.ap()[:, mi:mi + 1],
                                    accum_out=ml.ap()[:, 12 + mi:13 + mi]),
                                wait=(wt0 if mi == 0 else [])
                                + [("s_mx", mx0 + mi + 1)], inc="s_exp")
                    lbw.append(("s_exp", cnt))
                    mi += 1
                S.add("scalar",
                      lambda e, ml=ml, sc_=sc_, par=par, mi=mi,
                      dbase=dbase, doff=doff: e.activation(
                          out=ptil[par].ap()[:, dbase + doff:
                                             dbase + doff + 128],
                          in_=sc_.ap(), func=AF.Exp,
                          bias=ml.ap()[:, mi:mi + 1],
                          accum_out=ml.ap()[:, 12 + mi:13 + mi]),
                      wait=(wt0 if mi == 0 else [])
                      + [("s_mx", mx0 + mi + 1)], inc="s_exp")
                p2st["scb"][lb] = lbw
                r["exp"] = S.now("s_exp")
                S.add("scalar",
                      lambda e, ml=ml, nmx=nmx: e.activation(
                          out=ml.ap()[:, 18:18 + nmx],
                          in_=ml.ap()[:, 6:6 + nmx],
                          func=AF.Exp, bias=ml.ap()[:, 25:26]),
                      wait=[("s_mx", r["mxM"])], inc="s_fct")
                r["fct"] = S.now("s_fct")

            def stage_b_dve(i):
                tt, hh = units[i]
                par, ns, widths = rec[i]["par"], rec[i]["ns"], rec[i]["widths"]
                nmx = rec[i]["nmx"]
                doff = widths[-1] - 128
                ml = mxl[par]
                wls = [("s_fct", rec[i]["fct"])]
                if i >= 2:
                    wls.append(("s_psc", rec[i - 2]["psc"]))
                S.add("vector",
                      lambda e, ml=ml, nmx=nmx: e.tensor_mul(
                          ml.ap()[:, 12:12 + nmx], ml.ap()[:, 12:12 + nmx],
                          ml.ap()[:, 18:18 + nmx]),
                      wait=wls, inc="s_chv")
                S.add("vector",
                      lambda e, ml=ml, nmx=nmx: e.reduce_sum(
                          ml.ap()[:, 26:27], ml.ap()[:, 12:12 + nmx], AX.X),
                      wait=[("s_chv", S.now("s_chv"))], inc="s_chv")
                S.add("vector",
                      lambda e, ml=ml: e.reciprocal(ml.ap()[:, 27:28],
                                                    ml.ap()[:, 26:27]),
                      wait=[("s_chv", S.now("s_chv"))], inc="s_chv")
                S.add("vector",
                      lambda e, ml=ml, nmx=nmx: e.tensor_scalar(
                          ml.ap()[:, 18:18 + nmx], ml.ap()[:, 18:18 + nmx],
                          ml.ap()[:, 27:28], None, ALU.mult),
                      wait=[("s_chv", S.now("s_chv"))], inc="s_lrec")
                rec[i]["lrec"] = S.now("s_lrec")
                psc0 = S.now("s_psc")
                parts = []
                for s_ in range(ns - 1):
                    parts.append((SPW * s_, widths[s_]))
                dbase = SPW * (ns - 1)
                if doff > 0:
                    parts.append((dbase, doff))
                parts.append((dbase + doff, 128))
                for mi, (po, pw) in enumerate(parts):
                    wt = [("s_lrec", rec[i]["lrec"])] if mi == 0 else []
                    S.add("vector",
                          lambda e, ml=ml, mi=mi, po=po, pw=pw, par=par:
                          e.tensor_scalar(
                              ptil[par].ap()[:, po:po + pw],
                              ptil[par].ap()[:, po:po + pw],
                              ml.ap()[:, 18 + mi:19 + mi], None, ALU.mult),
                          wait=wt, inc="s_psc")
                rec[i]["psc"] = S.now("s_psc")
                rec[i]["psc_span"] = [psc0 + s_ + 1 for s_ in range(ns - 1)] \
                    + [psc0 + len(parts)]

            def stage_b(i):
                par, ns, widths = rec[i]["par"], rec[i]["ns"], rec[i]["widths"]
                psc_span = rec[i]["psc_span"]
                ev0 = S.now("s_ptev")
                rec[i]["ev0"] = ev0
                for s_ in range(ns):
                    w_ = widths[s_]
                    k_ = p2st["stg_ctr"]
                    stg = k_ % 2
                    p2st["stg_ctr"] += 1
                    bank = pb[6 + stg].ap().bitcast(BF16)
                    for j in range(w_ // 128):
                        wt = []
                        if j == 0:
                            wt = [("s_psc", psc_span[s_]),
                                  ("s_ptev", k_ - 1)]
                            if i == 0:
                                wt.append(("s_xT", nxt_done))
                        S.add("tensor",
                              lambda e, bank=bank, j=j, s_=s_, par=par:
                              e.transpose(
                                  bank[0:128, 128 * j:128 * j + 128],
                                  ptil[par].ap()[:, SPW * s_ + 128 * j:
                                                 SPW * s_ + 128 * j + 128],
                                  ident.ap()),
                              wait=wt,
                              inc="s_ptT" if j == w_ // 128 - 1 else None)
                    wt = [("s_ptT", S.now("s_ptT"))]
                    if s_ == 0 and i >= 2:
                        wt.append(("s_pvm", rec[i - 2]["pvm"]))
                    S.add("scalar",
                          lambda e, bank=bank, s_=s_, w_=w_, par=par:
                          e.activation(
                              out=ptT[par].ap()[:, SPW * s_:SPW * s_ + w_],
                              in_=bank[0:128, 0:w_], func=AF.Copy),
                          wait=wt, inc="s_ptev")
                rec[i]["ptT"] = S.now("s_ptT")

            def stage_c(i):
                tt, hh = units[i]
                par, ns, widths = rec[i]["par"], rec[i]["ns"], rec[i]["widths"]
                ev0 = rec[i]["ev0"]
                pvb = 4 + (i % 2)
                for c in range(tt + 1):
                    wt = []
                    if c % 4 == 0:
                        wt = [("s_ptev", ev0 + c // 4 + 1)]
                        if c == 0:
                            if i >= 2:
                                wt.append(("s_onev", rec[i - 2]["onev"]))
                            else:
                                wt.append(("s_vev", n_vev))
                    S.add("tensor",
                          lambda e, c=c, hh=hh, par=par, pvb=pvb, tt=tt:
                          e.matmul(
                              pb[pvb][0:64, 0:128],
                              Vsb.ap()[:, 1024 * c + 64 * hh:
                                       1024 * c + 64 * hh + 64],
                              ptT[par].ap()[:, 128 * c:128 * c + 128],
                              start=(c == 0), stop=(c == tt)),
                          wait=wt, inc="s_pvm" if c == tt else None)
                rec[i]["pvm"] = S.now("s_pvm")

            def stage_d(i):
                tt, hh = units[i]
                pvb = 4 + (i % 2)
                dck = hh // 2
                if hh % 2 == 0:
                    S.add("vector",
                          lambda e, pvb=pvb, dck=dck, tt=tt: e.tensor_copy(
                              On.ap()[0:64, T * dck + 128 * tt:
                                      T * dck + 128 * tt + 128],
                              pb[pvb][0:64, 0:128]),
                          wait=[("s_pvm", rec[i]["pvm"])], inc="s_onev")
                else:
                    o = p2st["odd_ctr"] % 2
                    p2st["odd_ctr"] += 1
                    S.add("vector",
                          lambda e, pvb=pvb, o=o: e.tensor_copy(
                              ostg[o].ap(), pb[pvb][0:64, 0:128]),
                          wait=[("s_pvm", rec[i]["pvm"]),
                                (f"s_onmv{o}", p2st["oddb"][o])],
                          inc="s_onev")
                    nv = S.now("s_onev")
                    p2st["oddb"][o] = S.dma(
                        "gpsimd",
                        On.ap()[64:128, T * dck + 128 * tt:
                                T * dck + 128 * tt + 128],
                        ostg[o].ap(), wait=[("s_onev", nv)],
                        inc=f"s_onmv{o}")
                rec[i]["onev"] = S.now("s_onev")

            for g in range(NU + 3):
                if 1 <= g < NU + 1:
                    stage_b_dve(g - 1)
                if g < NU:
                    stage_a(g)
                if 2 <= g < NU + 2:
                    stage_c(g - 2)
                if 1 <= g < NU + 1:
                    stage_b(g - 1)
                if 3 <= g:
                    stage_d(g - 3)
            n_onev = S.now("s_onev")
            n_onmv0 = S.now("s_onmv0")
            n_onmv1 = S.now("s_onmv1")
            if dbg and b == 0:
                wdbg = [("s_onev", n_onev), ("s_onmv0", n_onmv0),
                        ("s_onmv1", n_onmv1)]
                S.dma("scalar", d_dbg["dOn"], On.ap(), wait=wdbg, inc="s_dbg")
                S.dma("scalar", d_dbg["dptil"], ptil[1].ap(), wait=wdbg,
                      inc="s_dbg")
                S.dma("scalar", d_dbg["dptT"], ptT[1].ap(), wait=wdbg,
                      inc="s_dbg")
                S.dma("scalar", d_dbg["dmask"], negmask.ap(), wait=wdbg,
                      inc="s_dbg")
            bat_es.close()
            if stop == "p2":
                p23_es.close()
                return out_total

            p3_es = contextlib.ExitStack()
            x2b = [sb(p3_es, f"x2b{b}_{p}", [128, D], F32) for p in range(2)]
            xh2 = [sb(p3_es, f"xh2{b}_{p}", [128, D], BF16) for p in range(2)]
            xh2T = [sb(p3_es, f"xh2T{b}_{p}", [128, D], BF16) for p in range(2)]
            x3b = [sb(p3_es, f"x3b{b}_{p}", [128, D], F32) for p in range(2)]

            for tt in range(TT):
                par = tt % 2
                ln = lns[par]
                w = [("s_x2", S.now("s_x2") - 2), ("s_xhat", S.now("s_xhat")),
                     ("s_pvm", S.now("s_pvm"))]
                S.dma("sync", xt_in[par].ap(),
                      d_x[b, 128 * tt:128 * tt + 128, :], wait=w,
                      inc=f"s_x{par}")
                nx = S.now(f"s_x{par}")
                sx = f"s_x{par}"
                for fsp in range(2):
                    for dck in range(8):
                        w = []
                        if dck == 0:
                            w = [("s_onmv0", n_onmv0), ("s_onmv1", n_onmv1),
                                 ("s_onev", n_onev),
                                 ("s_wocast", n_wocast),
                                 ("s_x2", S.now("s_x2"))]
                        S.add("tensor",
                              lambda e, fsp=fsp, dck=dck, tt=tt, On=On:
                              e.matmul(
                                  pb[fsp][0:128, 0:512],
                                  On.ap()[:, T * dck + 128 * tt:
                                          T * dck + 128 * tt + 128],
                                  woutT_b.ap()[:, D * dck + 512 * fsp:
                                               D * dck + 512 * fsp + 512],
                                  start=(dck == 0), stop=False),
                              wait=w, inc=None)
                    S.add("tensor",
                          lambda e, fsp=fsp: e.matmul(
                              pb[fsp][0:128, 0:512], ones_bf.ap()[0:1, 0:128],
                              boutrow.ap()[0:1, 512 * fsp:512 * fsp + 512],
                              start=False, stop=True),
                          inc="s_op")
                nop = S.now("s_op")
                for fsp in range(2):
                    S.add("vector",
                          lambda e, fsp=fsp, par=par: e.tensor_add(
                              x2b[par].ap()[:, 512 * fsp:512 * fsp + 512],
                              xt_in[par].ap()[:, 512 * fsp:512 * fsp + 512],
                              pb[fsp][0:128, 0:512]),
                          wait=[("s_op", nop), (sx, nx),
                                ("s_xh2", S.now("s_xh2") - 1)], inc="s_x2")
                nx2 = S.now("s_x2")
                S.add("scalar",
                      lambda e, par=par, ln=ln: e.activation(
                          out=sq_junk.ap(), in_=x2b[par].ap(), func=AF.Square,
                          accum_out=ln.ap()[:, 1:2]),
                      wait=[("s_x2", nx2), ("s_lnv", S.now("s_lnv") - 1)],
                      inc="s_stats")
                S.add("vector",
                      lambda e, par=par, ln=ln: e.reduce_sum(
                          ln.ap()[:, 0:1], x2b[par].ap(), AX.X),
                      wait=[("s_x2", nx2), ("s_lnr", S.now("s_lnr") - 1)],
                      inc="s_stats")
                emit_ln_smalls(S, ln)
                S.add("scalar",
                      lambda e, ln=ln: e.activation(
                          out=ln.ap()[:, 6:7], in_=ln.ap()[:, 5:6], func=AF.Ln),
                      wait=[("s_lnv", S.now("s_lnv"))], inc="s_chs")
                S.add("scalar",
                      lambda e, ln=ln: e.activation(
                          out=ln.ap()[:, 7:8], in_=ln.ap()[:, 6:7], func=AF.Exp,
                          scale=-0.5),
                      wait=[("s_chs", S.now("s_chs"))], inc="s_lnr")
                S.add("vector",
                      lambda e, ln=ln: e.tensor_mul(
                          ln.ap()[:, 9:10], ln.ap()[:, 8:9], ln.ap()[:, 7:8]),
                      wait=[("s_lnr", S.now("s_lnr")),
                            ("s_xh2", S.now("s_xh2") - 1)], inc="s_lnm")
                S.add("scalar",
                      lambda e, par=par, ln=ln: e.activation(
                          out=xh2[par].ap(), in_=x2b[par].ap(),
                          func=AF.Identity,
                          bias=ln.ap()[:, 9:10], scale=ln.ap()[:, 7:8]),
                      wait=[("s_lnm", S.now("s_lnm")),
                            ("s_fTmm", S.now("s_fTmm") - 8)],
                      inc="s_xh2")
                nxh2 = S.now("s_xh2")
                for half2 in range(2):
                    bank = pb[2 + half2].ap().bitcast(BF16)
                    for j in range(4):
                        dc = 4 * half2 + j
                        wtr = []
                        if j == 0:
                            wtr = [("s_xh2", nxh2),
                                   ("s_fT", S.now("s_fT") - 1)]
                        S.add("tensor",
                              lambda e, par=par, dc=dc, bank=bank, j=j:
                              e.transpose(bank[0:128, 128 * j:128 * j + 128],
                                          xh2[par].ap()[:, 128 * dc:128 * dc + 128],
                                          ident.ap()),
                              wait=wtr, inc="s_fTmm")
                    S.add("vector",
                          lambda e, par=par, half2=half2, bank=bank:
                          e.tensor_copy(
                              xh2T[par].ap()[:, 512 * half2:512 * half2 + 512],
                              bank[0:128, 0:512]),
                          wait=[("s_fTmm", S.now("s_fTmm")),
                                ("s_ffn", S.now("s_ffn") - 2)], inc="s_fT")
                nfT = S.now("s_fT")
                for half in range(2):
                    for ic in range(8):
                        if half == 0:
                            wmat = wt_r if ic < 4 else wt_ni
                        else:
                            wmat = wt_i if ic < 4 else wt_r
                        wcol = 512 * (ic % 4)
                        w = []
                        if ic == 0:
                            w = [("s_fT", nfT), ("s_cst", setup_done["cst"]),
                                 ("s_x3", S.now("s_x3"))]
                        S.add("tensor",
                              lambda e, half=half, ic=ic, par=par, wmat=wmat,
                              wcol=wcol: e.matmul(
                                  pb[4 + half][0:128, 0:512],
                                  xh2T[par].ap()[:, 128 * ic:128 * ic + 128],
                                  wmat.ap()[:, wcol:wcol + 512],
                                  start=(ic == 0), stop=False),
                              wait=w, inc=None)
                    S.add("tensor",
                          lambda e, half=half: e.matmul(
                              pb[4 + half][0:128, 0:512],
                              ones_bf.ap()[0:1, 0:128],
                              fbrow.ap()[0:1, 512 * half:512 * half + 512],
                              start=False, stop=True),
                          inc="s_ffn")
                nffn = S.now("s_ffn")
                for half in range(2):
                    S.add("vector",
                          lambda e, half=half, par=par: e.tensor_add(
                              x3b[par].ap()[:, 512 * half:512 * half + 512],
                              x2b[par].ap()[:, 512 * half:512 * half + 512],
                              pb[4 + half][0:128, 0:512]),
                          wait=[("s_ffn", nffn),
                                (f"s_out{par}", S.now(f"s_out{par}"))],
                          inc="s_x3")
                S.dma("gpsimd", d_y[b, 128 * tt:128 * tt + 128, :],
                      x3b[par].ap(), wait=[("s_x3", S.now("s_x3"))],
                      inc=f"s_out{par}")
            out_total += TT
            p3_es.close()
            p23_es.close()
            return out_total

        p2st = {"sc_ctr": 0, "stg_ctr": 0, "odd_ctr": 0,
                "scb": [[], [], [], []], "oddb": [0, 0]}

        out_total = 0
        if stop != "setup":
            for b in range(NB):
                out_total = emit_batch(b, out_total)

        f0, f1 = S.now("s_out0"), S.now("s_out1")
        fd = S.now("s_dbg")
        S.q["gpsimd"].append(
            (lambda e: (e.wait_ge(S.sem("s_out0"), f0),
                        e.wait_ge(S.sem("s_dbg"), fd) if fd else None,
                        e.wait_ge(S.sem("s_out1"), f1))[-1], [], None))

        block = es.enter_context(nc.Block())
        S.emit(block)
    return nc


def emit_ln_smalls(S, ln):
    a = ln.ap()
    S.add("vector", lambda e: e.tensor_scalar_mul(a[:, 2:3], a[:, 0:1], 1.0 / D),
          wait=[("s_stats", S.now("s_stats"))], inc="s_chv")
    S.add("vector", lambda e: e.tensor_scalar(a[:, 4:5], a[:, 1:2], 1.0 / D,
                                              None, ALU.mult))
    S.add("vector", lambda e: e.tensor_mul(a[:, 3:4], a[:, 2:3], a[:, 2:3]),
          wait=[("s_chv", S.now("s_chv"))], inc="s_chv")
    S.add("vector", lambda e: e.tensor_scalar_mul(a[:, 8:9], a[:, 2:3], -1.0))
    S.add("vector", lambda e: e.tensor_scalar(a[:, 5:6], a[:, 4:5], a[:, 3:4],
                                              EPS, ALU.subtract, ALU.add),
          wait=[("s_chv", S.now("s_chv"))], inc="s_lnv")



def _prep_weights(w_qkv, b_qkv, w_out, G1_R, G2_R, G1_I, G2_I):
    perm = _qk_perm()
    wp = np.ascontiguousarray(np.asarray(w_qkv)[perm])
    bp = np.ascontiguousarray(np.asarray(b_qkv)[perm])
    wqkvT = np.ascontiguousarray(wp.T)
    bcols = np.ascontiguousarray(bp[:2048].reshape(16, 128).T)
    brow = bp.reshape(1, 3 * D)
    woutT = np.ascontiguousarray(np.asarray(w_out).T)
    g1 = {"r": np.ascontiguousarray(np.asarray(G1_R)[0].transpose(1, 2, 0)),
          "i": np.ascontiguousarray(np.asarray(G1_I)[0].transpose(1, 2, 0))}
    g2 = {"r": np.ascontiguousarray(np.asarray(G2_R)[:, :, 0, :].transpose(0, 2, 1)),
          "i": np.ascontiguousarray(np.asarray(G2_I)[:, :, 0, :].transpose(0, 2, 1))}
    return wqkvT, bcols, brow, woutT, g1, g2


def kernel(L0_attn_x, L0_attn_w_qkv, L0_attn_b_qkv, L0_attn_w_out, L0_attn_b_out,
           L0_ffn_G1_R, L0_ffn_G2_R, L0_ffn_G1_I, L0_ffn_G2_I,
           L0_ffn_bias_real, L0_ffn_bias_imag):
    x = np.ascontiguousarray(np.asarray(L0_attn_x, dtype=np.float32))
    B, T, _ = x.shape
    NB = B // NCORES
    wqkvT, bcols, brow, woutT, g1, g2 = _prep_weights(
        L0_attn_w_qkv, L0_attn_b_qkv, L0_attn_w_out,
        L0_ffn_G1_R, L0_ffn_G2_R, L0_ffn_G1_I, L0_ffn_G2_I)
    consts = _host_consts(T)
    shared = {
        "wqkvT": wqkvT.astype(np.float32),
        "bcols": bcols.astype(np.float32),
        "brow": brow.astype(np.float32),
        "woutT": woutT.astype(np.float32),
        "bout": np.asarray(L0_attn_b_out, np.float32).reshape(1, D),
        "g1r": g1["r"].astype(np.float32), "g1i": g1["i"].astype(np.float32),
        "g2r": g2["r"].astype(np.float32), "g2i": g2["i"].astype(np.float32),
        "fbr": np.asarray(L0_ffn_bias_real, np.float32).reshape(1, 512),
        "fbi": np.asarray(L0_ffn_bias_imag, np.float32).reshape(1, 512),
    }
    for k, v in consts.items():
        shared[k] = v
    nc = build_nc(NB, T)
    in_maps = []
    for c in range(NCORES):
        m = dict(shared)
        m["x"] = np.ascontiguousarray(x[NB * c:NB * (c + 1)])
        in_maps.append(m)
    res = run_bass_kernel_spmd(nc, in_maps, core_ids=list(range(NCORES)))
    return np.concatenate([np.asarray(r["y"]) for r in res.results], axis=0)
